# version 02 (final)
# baseline (speedup 1.0000x reference)
import numpy as np
from contextlib import ExitStack
import concourse.bass as bass
import concourse.mybir as mybir
from concourse.bass_utils import run_bass_kernel_spmd

F32 = mybir.dt.float32
BF16 = mybir.dt.bfloat16
I32 = mybir.dt.int32
ALU = mybir.AluOpType
AF = mybir.ActivationFunctionType
AX = mybir.AxisListType.X

NCORES = 8
T = 8192
TL = 1024
D = 2048
NF = 16
H = 16
HD = 128
NI0 = 8192
MAGIC = 1.5 * 2 ** 23
INVSQ = float(1.0 / np.sqrt(128.0))
EXPSHIFT = -20.0
RG = [list(range(NCORES))]
DBG6 = False
DBG8 = False
STAGE = 1
KCUT = 8
RACE_DETECT = True


class S:
    def __init__(self, h):
        self.h = h
        self.n = 0

    def bump(self, k=1):
        self.n += k
        return self.n


def build(taps=()):
    nc = bass.Bass("TRN2", target_bir_lowering=False, dynamic_dma_scratch_size=16384,
                   detect_race_conditions=RACE_DETECT)
    taps = set(taps)

    def di(name, shape, dt=F32):
        return nc.dram_tensor(name, shape, dt, kind="ExternalInput").ap()

    x_tok = di("x_tok", [TL, D])
    l0wq = di("l0wq", [NF, 128, D]); l0wk = di("l0wk", [NF, 128, D])
    l0wv = di("l0wv", [NF, 128, D]); l0wo = di("l0wo", [NF, 128, D])
    l0bq = di("l0bq", [128, NF]); l0bk = di("l0bk", [128, NF])
    l0bv = di("l0bv", [128, NF]); l0bo = di("l0bo", [128, NF])
    l0c_fm = di("l0c_fm", [128, NF]); l0c_s = di("l0c_s", [128, 2])
    l1c_fm = di("l1c_fm", [128, NF]); l1c_s = di("l1c_s", [128, 2])
    wmod0_s = di("wmod0_s", [256, 6 * D])
    wmod1_s = di("wmod1_s", [256, 6 * D])
    l0wg_s = di("l0wg_s", [256, NI0]); l0wu_s = di("l0wu_s", [256, NI0])
    l0wd_s = di("l0wd_s", [1024, D])
    l1wq = di("l1wq", [NF, 128, D]); l1wo = di("l1wo", [NF, 128, D])
    l1wk = di("l1wk", [4, 128, D]); l1wv = di("l1wv", [4, 128, D])
    l1gp_s = di("l1gp_s", [1024, 2048], I32)
    l1up_s = di("l1up_s", [1024, 2048], I32)
    l1dp_s = di("l1dp_s", [1024, 2048], I32)
    l1gs_s = di("l1gs_s", [16, 128, 1024]); l1us_s = di("l1us_s", [16, 128, 1024])
    l1ds_s = di("l1ds_s", [16, 128, 1024])
    pmask = di("pmask", [128, NCORES])
    t_off = di("t_off", [128, 1])

    y = nc.dram_tensor("y", [TL, D], F32, kind="ExternalOutput").ap()
    if DBG6:
        t_aq = nc.dram_tensor("t_aq", [128, NF * TL], BF16, kind="ExternalOutput").ap()
        t_int = nc.dram_tensor("t_int", [64, 128, TL], BF16, kind="ExternalOutput").ap()
        t_scl = nc.dram_tensor("t_scl", [128, 8], F32, kind="ExternalOutput").ap()
        t_h128 = nc.dram_tensor("t_h128", [128, 128], BF16, kind="ExternalOutput").ap()
        t_hb = nc.dram_tensor("t_hb", [128, NF * TL], BF16, kind="ExternalOutput").ap()
    tap_out = {}
    for tp in taps:
        tap_out[tp] = nc.dram_tensor("tap_" + tp, [128, NF * TL], F32,
                                     kind="ExternalOutput").ap()

    if DBG8:
        t4 = {}
        for nm, shp in (("hb", [2048, TL]), ("gw", [16, 16, 128, 128]),
                        ("uw", [16, 16, 128, 128]), ("dw", [16, 16, 128, 128]),
                        ("p0", [512, TL]), ("p1", [512, TL]),
                        ("p2", [512, TL]), ("p3", [512, TL]),
                        ("pi", [NCORES * 512, TL])):
            t4[nm] = nc.dram_tensor("t4_" + nm, shp, BF16, kind="ExternalOutput").ap()

    ar_in = nc.dram_tensor("ar_in", [128, 195], F32)
    ar_out = nc.dram_tensor("ar_out", [128, 195], F32)
    scal_dram = nc.dram_tensor("scal_dram", [1, 8], F32)
    ksh = nc.dram_tensor("ksh", [D, TL], BF16)
    vsh = nc.dram_tensor("vsh", [H * 8 * 128, 128], BF16)
    kfull = nc.dram_tensor("kfull", [NCORES, D, TL], BF16)
    vfull = nc.dram_tensor("vfull", [NCORES, H * 8 * 128, 128], BF16)
    inter_dram = nc.dram_tensor("inter_dram", [64, 128, TL], BF16)
    fqg_sh = nc.dram_tensor("fqg_sh", [64, 2, 128, 128], BF16)
    fqu_sh = nc.dram_tensor("fqu_sh", [64, 2, 128, 128], BF16)
    fqd_sh = nc.dram_tensor("fqd_sh", [8, 2, 8, 128, 128], BF16)
    fqg_full = nc.dram_tensor("fqg_full", [NCORES, 64, 2, 128, 128], BF16)
    fqu_full = nc.dram_tensor("fqu_full", [NCORES, 64, 2, 128, 128], BF16)
    fqd_full = nc.dram_tensor("fqd_full", [NCORES, 8, 2, 8, 128, 128], BF16)
    sz_sh = nc.dram_tensor("sz_sh", [512, 129], F32)
    sz_full = nc.dram_tensor("sz_full", [NCORES, 512, 129], F32)
    hb_sh = nc.dram_tensor("hb_sh", [2048, TL], BF16)
    hb_full = nc.dram_tensor("hb_full", [NCORES, 2048, TL], BF16)
    l1gw = nc.dram_tensor("l1gw", [16, 16, 128, 128], BF16)
    l1uw = nc.dram_tensor("l1uw", [16, 16, 128, 128], BF16)
    l1dw = nc.dram_tensor("l1dw", [16, 16, 128, 128], BF16)
    l1p_in = [nc.dram_tensor("l1p_in%d" % s, [NCORES, 512, TL], BF16)
              for s in range(4)]
    l1p_out = [nc.dram_tensor("l1p_out%d" % s, [512, TL], BF16)
               for s in range(4)]

    E = {k: [] for k in ("tensor", "vector", "scalar", "gpsimd", "sync")}

    def em(eng, f):
        E[eng].append(f)

    stack = ExitStack()
    sems = {}

    def sem(name):
        if name not in sems:
            sems[name] = S(stack.enter_context(nc.semaphore(name)))
        return sems[name]

    def sb(name, shape, dt):
        return stack.enter_context(nc.sbuf_tensor(name, shape, dt))

    def psum(name, shape, dt=F32):
        return stack.enter_context(nc.psum_tensor(name, shape, dt))

    with stack:
        X = sb("X", [128, NF * TL], F32)
        hpan = sb("hpan", [128, NF * TL], BF16)
        qarena = sb("qarena", [128, NF * TL], BF16)
        scr = sb("scr", [128, 8192], F32)
        scrb = scr[:].bitcast(BF16)

        ident = sb("ident", [128, 128], BF16)
        identf = sb("identf", [128, 128], F32)
        ones_col = sb("ones_col", [128, 1], BF16)
        ones_colf = sb("ones_colf", [128, 1], F32)
        ones_row = sb("ones_row", [1, 128], BF16)
        tri = sb("tri", [128, 128], BF16)
        biasq = sb("biasq", [128, NF], F32)
        biask = sb("biask", [128, NF], F32)
        biasv = sb("biasv", [128, NF], F32)
        biaso = sb("biaso", [128, NF], F32)
        arres = sb("arres", [128, 195], F32)
        modv = sb("modv", [128, 272], F32)
        scl = sb("scl", [128, 8], F32)
        pm = sb("pm", [128, NCORES], F32)
        toff = sb("toff", [128, 1], F32)
        cfm0 = sb("cfm0", [128, NF], BF16); cfm0s = sb("cfm0s", [128, 2], F32)
        cfm1 = sb("cfm1", [128, NF], BF16); cfm1s = sb("cfm1s", [128, 2], F32)
        xb = sb("xb", [128, TL], BF16)
        xsq = sb("xsq", [128, TL], BF16)
        bcD = xsq
        srow0 = sb("srow0", [128, TL], F32)
        srow1 = sb("srow1", [128, TL], F32)
        rrowb = sb("rrowb", [128, TL], BF16)
        shared4k = sb("shared4k", [128, 2 * TL], BF16)
        bcA = shared4k[:, 0:TL]
        bcB = shared4k[:, TL:2 * TL]
        den = sb("den", [128, TL], BF16)
        big8 = sb("big8", [128, D], F32)
        qin = sb("qin", [128, TL], F32)
        qstage = sb("qstage", [128, TL], BF16)
        h128 = sb("h128", [128, 128], BF16)
        hint = sb("hint", [128, 128], I32)
        hint2 = sb("hint2", [128, 128], I32)
        introw = sb("introw", [128, 1], I32)
        mwt = big8[:]
        big8b = big8[:].bitcast(BF16)
        pbuf = [big8b[:, 0:1024], big8b[:, 1024:2048]]
        vtout = big8b[:, 2048:3072]
        mred = sb("mred", [128, 26], F32)
        wec = [sb("wec0", [128, NF * 128], BF16), sb("wec1", [128, NF * 128], BF16)]
        krot = shared4k
        vrot = krot
        xout = scr[:, 0:D]

        pA = psum("pA", [128, 1024], F32)
        pB = psum("pB", [128, 1024], F32)
        pC = psum("pC", [128, 1024], F32)
        pD = psum("pD", [128, 512], F32)
        pE = psum("pE", [128, 512], F32)

        cb = sb("cb", [128, 8], F32)
        CB_VALS = [EXPSHIFT, 1e-5, -8.0, -10.0, -7.5, 0.5, 1.0, -4.0]
        for i, vv in enumerate(CB_VALS):
            nc.const_aps.register(F32, float(vv), cb[:, i:i + 1])

        def cbias(val):
            return cb[:, CB_VALS.index(val):CB_VALS.index(val) + 1]

        sConst = sem("sConst"); sGP = sem("sGP"); sX = sem("sX")
        sPE = sem("sPE"); sDV = sem("sDV"); sAC = sem("sAC")
        sWM = sem("sWM"); sMean = sem("sMean"); sAR = sem("sAR")
        sW = sem("sW"); sKVp = sem("sKVp"); sKV = sem("sKV")
        sFL = sem("sFL"); sST = sem("sST")
        sXb = sem("sXb"); sWb = sem("sWb"); sKPb = sem("sKPb")
        sFLb = sem("sFLb"); sFLV = sem("sFLV")

        def p_const(g):
            g.memset(ident[:], 0.0).then_inc(sGP.h, 1)
            g.memset(identf[:], 0.0).then_inc(sGP.h, 1)
            g.memset(ones_col[:], 1.0).then_inc(sGP.h, 1)
            g.memset(ones_colf[:], 1.0).then_inc(sGP.h, 1)
            for i, vv in enumerate(CB_VALS):
                g.memset(cb[:, i:i + 1], float(vv)).then_inc(sGP.h, 1)
            g.memset(ones_row[:], 1.0).then_inc(sGP.h, 1)
            g.memset(tri[:], 1.0).then_inc(sGP.h, 1)
            g.wait_ge(sGP.h, 14)
            g.affine_select(ident[:], ident[:], pattern=[[-1, 128]],
                            compare_op=ALU.not_equal, fill=1.0, base=0,
                            channel_multiplier=1).then_inc(sGP.h, 1)
            g.affine_select(identf[:], identf[:], pattern=[[-1, 128]],
                            compare_op=ALU.not_equal, fill=1.0, base=0,
                            channel_multiplier=1).then_inc(sGP.h, 1)
            g.affine_select(tri[:], tri[:], pattern=[[1, 128]],
                            compare_op=ALU.is_ge, fill=0.0, base=0,
                            channel_multiplier=-1).then_inc(sGP.h, 1)
            g.iota(hint[:], pattern=[[1, 128]], base=0, channel_multiplier=0,
                   allow_small_or_imprecise_dtypes=True).then_inc(sGP.h, 1)
            g.iota(introw[:], pattern=[[0, 1]], base=0, channel_multiplier=1,
                   allow_small_or_imprecise_dtypes=True).then_inc(sGP.h, 1)
            for dst, src in ((biasq, l0bq), (biask, l0bk), (biasv, l0bv),
                             (biaso, l0bo), (pm, pmask), (toff, t_off)):
                g.dma_start(out=dst[:], in_=src).then_inc(sConst.h, 16)
            for dst, src in ((cfm0, l0c_fm), (cfm0s, l0c_s),
                             (cfm1, l1c_fm), (cfm1s, l1c_s)):
                g.dma_start(out=dst[:], in_=src).then_inc(sConst.h, 16)
        em("gpsimd", p_const)
        sGP.bump(19)
        GP_CONST = sGP.n
        CONST_DONE = sConst.bump(16 * 10)

        def p_h128(v, B=sDV.n):
            v.wait_ge(sGP.h, GP_CONST)
            k = 1
            v.tensor_scalar(hint[:], hint[:], introw[:, 0:1], None,
                            ALU.bitwise_and).then_inc(sDV.h, 1)
            for sh in (4, 2, 1):
                v.wait_ge(sDV.h, B + k)
                v.tensor_scalar(hint2[:], hint[:], sh, None,
                                ALU.logical_shift_right).then_inc(sDV.h, 1)
                v.wait_ge(sDV.h, B + k + 1)
                v.tensor_tensor(hint[:], hint[:], hint2[:],
                                ALU.bitwise_xor).then_inc(sDV.h, 1)
                k += 2
            v.wait_ge(sDV.h, B + 7)
            v.tensor_scalar(hint[:], hint[:], 1, None, ALU.bitwise_and).then_inc(sDV.h, 1)
            v.wait_ge(sDV.h, B + 8)
            v.tensor_scalar(h128[:], hint[:], -2, 1, ALU.mult, ALU.add).then_inc(sDV.h, 1)
        em("vector", p_h128)
        H128_DONE = sDV.bump(9)

        def mk_xload(tb):
            ss = sX if tb % 2 == 0 else sXb

            def f(e):
                if tb >= 2:
                    e.wait_ge(sPE.h, XTR[tb - 2])
                e.dma_start(out=scr[:, (tb % 2) * D:(tb % 2) * D + D],
                            in_=x_tok[tb * 128:(tb + 1) * 128, :]).then_inc(ss.h, 16)
            return f

        XTR = []
        xld_dv0 = sDV.n
        for tb in range(8):
            em("sync", mk_xload(tb))
            ssx = sX if tb % 2 == 0 else sXb
            ssx.bump(16)
            x_at = ssx.n

            def mk_xtr(tb, x_at, dv_need, ssx=ssx):
                def f(t):
                    t.wait_ge(ssx.h, x_at)
                    if tb == 0:
                        t.wait_ge(sGP.h, GP_CONST)
                    if tb >= 1:
                        t.wait_ge(sDV.h, dv_need)
                    for g8 in range(2):
                        pp = pA if g8 == 0 else pB
                        for fc in range(8):
                            fcg = g8 * 8 + fc
                            ins = t.transpose(
                                pp[:, fc * 128:(fc + 1) * 128],
                                scr[:, (tb % 2) * D + fcg * 128:(tb % 2) * D + fcg * 128 + 128],
                                identf[:])
                        ins.then_inc(sPE.h, 1)
                return f
            em("tensor", mk_xtr(tb, x_at, xld_dv0 + tb))
            pe_at = sPE.bump(2)

            def mk_xev(tb, pe_at):
                def f(v):
                    v.wait_ge(sPE.h, pe_at - 1)
                    for fc in range(8):
                        v.tensor_copy(X[:, fc * TL + tb * 128: fc * TL + tb * 128 + 128],
                                      pA[:, fc * 128:(fc + 1) * 128])
                    v.wait_ge(sPE.h, pe_at)
                    for fc in range(8):
                        ins = v.tensor_copy(
                            X[:, (8 + fc) * TL + tb * 128:(8 + fc) * TL + tb * 128 + 128],
                            pB[:, fc * 128:(fc + 1) * 128])
                    ins.then_inc(sDV.h, 1)
                return f
            em("vector", mk_xev(tb, pe_at))
            sDV.bump(1)
            XTR.append(sPE.n)
        X_READY = sDV.n

        def p_siluc(s):
            s.wait_ge(sConst.h, CONST_DONE)
            s.activation(cfm0[:], cfm0[:], AF.Silu)
            s.activation(cfm0s[:], cfm0s[:], AF.Silu)
            s.activation(cfm1[:], cfm1[:], AF.Silu)
            s.activation(cfm1s[:], cfm1s[:], AF.Silu).then_inc(sAC.h, 1)
        em("scalar", p_siluc)
        AC_SILUC = sAC.bump(1)

        wmpf = qarena[:].bitcast(F32)[:, 0:3 * D]
        arbuf = den[:].bitcast(F32)[:, 0:195]
        wm_specs = [(idx, [wmod0_s, wmod0_s, wmod0_s, wmod0_s,
                           wmod1_s, wmod1_s, wmod1_s, wmod1_s][idx],
                     (idx % 4) // 2, idx % 2) for idx in range(8)]
        wm_pe = []
        wm_dv = []
        for idx, src, kch, mh in wm_specs:
            def mk_wml(idx, src, kch, mh):
                def f(g):
                    if idx > 0:
                        g.wait_ge(sPE.h, wm_pe[idx - 1])
                    g.dma_start(out=wmpf,
                                in_=src[kch * 128:(kch + 1) * 128,
                                        mh * 3 * D:(mh + 1) * 3 * D]).then_inc(sWM.h, 16)
                return f
            em("gpsimd", mk_wml(idx, src, kch, mh))
            sWM.bump(16)
            wm_at = sWM.n
            L = idx // 4

            def mk_wmm(idx, kch, mh, L, wm_at):
                cs = cfm0s if L == 0 else cfm1s
                def f(t):
                    t.wait_ge(sWM.h, wm_at)
                    t.wait_ge(sAC.h, AC_SILUC)
                    if idx == 4:
                        t.wait_ge(sDV.h, wm_dv[0])
                    for m in range(48):
                        ins = t.matmul(pC[:, mh * 48 + m:mh * 48 + m + 1],
                                       wmpf[:, m * 128:(m + 1) * 128],
                                       cs[:, kch:kch + 1],
                                       start=(idx % 4 == 0 and m == 0),
                                       stop=(idx % 4 == 3 and m == 47),
                                       skip_group_check=True)
                    ins.then_inc(sPE.h, 1)
                return f
            em("tensor", mk_wmm(idx, kch, mh, L, wm_at))
            wm_pe.append(sPE.bump(1))
            if idx % 4 == 3:
                def mk_wmev(L, pe_at):
                    def f(v):
                        v.wait_ge(sPE.h, pe_at)
                        v.tensor_copy(arbuf[:, L * 96:L * 96 + 96],
                                      pC[:, 0:96]).then_inc(sDV.h, 1)
                    return f
                em("vector", mk_wmev(L, wm_pe[-1]))
                wm_dv.append(sDV.bump(1))
        WM_EVAC = sDV.n

        mean_specs = []
        for r in range(2):
            for cc in range(4):
                mean_specs.append(l0wg_s[r * 128:(r + 1) * 128, cc * D:(cc + 1) * D])
        for r in range(2):
            for cc in range(4):
                mean_specs.append(l0wu_s[r * 128:(r + 1) * 128, cc * D:(cc + 1) * D])
        for r in range(8):
            mean_specs.append(l0wd_s[r * 128:(r + 1) * 128, :])
        mean_dv0 = sDV.n
        for i, src in enumerate(mean_specs):
            def mk_ml(i, src):
                def f(e):
                    if i >= 1:
                        e.wait_ge(sDV.h, mean_dv0 + i)
                    e.dma_start(out=mwt[:], in_=src).then_inc(sMean.h, 16)
                return f
            em("sync", mk_ml(i, src))
            sMean.bump(16)
            m_at = sMean.n

            def mk_mr(i, m_at):
                def f(v):
                    v.wait_ge(sMean.h, m_at)
                    v.tensor_reduce(mred[:, i:i + 1], mwt[:], AX, ALU.add,
                                    apply_absolute_value=True).then_inc(sDV.h, 1)
                return f
            em("vector", mk_mr(i, m_at))
            sDV.bump(1)
        MEAN_RED = sDV.n

        def p_mcomb(v):
            v.wait_ge(sDV.h, MEAN_RED)
            for p4 in range(4):
                for j in range(3):
                    v.tensor_add(mred[:, j * 8 + p4:j * 8 + p4 + 1],
                                 mred[:, j * 8 + 2 * p4:j * 8 + 2 * p4 + 1],
                                 mred[:, j * 8 + 2 * p4 + 1:j * 8 + 2 * p4 + 2])
            for p2 in range(2):
                for j in range(3):
                    v.tensor_add(mred[:, j * 8 + p2:j * 8 + p2 + 1],
                                 mred[:, j * 8 + 2 * p2:j * 8 + 2 * p2 + 1],
                                 mred[:, j * 8 + 2 * p2 + 1:j * 8 + 2 * p2 + 2])
            for j, col in ((0, 192), (1, 193), (2, 194)):
                v.tensor_add(arbuf[:, col:col + 1], mred[:, j * 8:j * 8 + 1],
                             mred[:, j * 8 + 1:j * 8 + 2]).then_inc(sDV.h, 1)
        em("vector", p_mcomb)
        AR_READY = sDV.bump(3)

        def p_ar(g):
            g.wait_ge(sDV.h, max(AR_READY, WM_EVAC))
            g.dma_start(out=ar_in[:, :], in_=arbuf[:]).then_inc(sConst.h, 16)
            g.wait_ge(sConst.h, CONST_DONE + 16)
            g.collective_compute("AllReduce", ALU.add, replica_groups=RG,
                                 ins=[ar_in.ap()], outs=[ar_out.ap()]).then_inc(sAR.h, 1)
            g.wait_ge(sAR.h, 1)
            g.dma_start(out=arres[:], in_=ar_out[:, :]).then_inc(sConst.h, 16)
        em("gpsimd", p_ar)
        sAR.bump(1)
        ARRES_LOADED = sConst.bump(32)

        def p_modderive(v):
            v.wait_ge(sConst.h, ARRES_LOADED)
            for L in range(2):
                b = L * 96
                v.tensor_copy(modv[:, b + 0:b + 16], arres[:, b + 0:b + 16])
                v.tensor_copy(modv[:, b + 48:b + 64], arres[:, b + 48:b + 64])
                v.tensor_scalar(modv[:, b + 16:b + 32], arres[:, b + 16:b + 32], 1.0, None, ALU.add)
                v.tensor_scalar(modv[:, b + 64:b + 80], arres[:, b + 64:b + 80], 1.0, None, ALU.add)
            gates = [(0, 32), (0, 80), (1, 32), (1, 80)]
            for i, (L, off) in enumerate(gates):
                sc = 192 + 16 * i
                v.tensor_scalar(modv[:, sc:sc + 16], arres[:, L * 96 + off:L * 96 + off + 16],
                                0.0, 2.0, ALU.is_ge, ALU.mult)
            for i, (L, off) in enumerate(gates):
                sc = 192 + 16 * i
                v.tensor_scalar(modv[:, sc:sc + 16], modv[:, sc:sc + 16], -1.0, None, ALU.add)
            for i, (L, off) in enumerate(gates):
                g = L * 96 + off
                v.scalar_tensor_tensor(modv[:, g:g + 16], arres[:, g:g + 16], -1.0,
                                       arres[:, g:g + 16], ALU.mult, ALU.max)
            for i, (L, off) in enumerate(gates):
                g = L * 96 + off
                v.tensor_scalar(modv[:, g:g + 16], modv[:, g:g + 16], 0.25, None, ALU.max)
            for i, (L, off) in enumerate(gates):
                g = L * 96 + off
                sc = 192 + 16 * i
                ins = v.tensor_mul(modv[:, g:g + 16], modv[:, g:g + 16], modv[:, sc:sc + 16])
            ins.then_inc(sDV.h, 1)
        em("vector", p_modderive)
        MODV_DONE = sDV.bump(1)

        def p_meansum(t):
            t.wait_ge(sConst.h, ARRES_LOADED)
            t.matmul(pD[0:1, 0:3], ones_colf[:], arres[:, 192:195],
                     start=True, stop=True).then_inc(sPE.h, 1)
        em("tensor", p_meansum)
        PE_MEANS = sPE.bump(1)

        def p_mscal_a(v):
            v.wait_ge(sPE.h, PE_MEANS)
            v.tensor_scalar(mred[0:1, 3:6], pD[0:1, 0:3], 1.0 / (NI0 * D), None,
                            ALU.mult).then_inc(sDV.h, 1)
            v.wait_ge(sDV.h, MSC_A0)
            v.tensor_scalar(mred[0:1, 3:6], mred[0:1, 3:6], 1e-5, None, ALU.max).then_inc(sDV.h, 1)
        MSC_A0 = sDV.n + 1
        em("vector", p_mscal_a)
        MSC_A = sDV.bump(2)

        def p_mscal_b(v):
            v.wait_ge(sDV.h, MSC_A)
            v.reciprocal(mred[0:1, 6:9], mred[0:1, 3:6]).then_inc(sDV.h, 1)
        em("vector", p_mscal_b)
        MSC_B = sDV.bump(1)

        def p_mscal_c(g):
            g.wait_ge(sDV.h, MSC_B)
            g.dma_start(out=scal_dram[0:1, 0:6], in_=mred[0:1, 3:9]).then_inc(sConst.h, 16)
            g.wait_ge(sConst.h, ARRES_LOADED + 16)
            g.dma_start(out=scl[:, 0:6],
                        in_=scal_dram[0:1, 0:6].broadcast_to((128, 6))).then_inc(sConst.h, 16)
        em("gpsimd", p_mscal_c)
        SCL_DONE = sConst.bump(32)

        ln_state = {"prev_stat_free": 0}

        def ln_site(x_ready_dv, ca_col, sa_col, out_pan):
            pe0 = sPE.n
            prev_free = ln_state["prev_stat_free"]
            for fc in range(NF):
                def mk_xb(fc, pe_need):
                    def f(v):
                        if fc == 0:
                            v.wait_ge(sDV.h, x_ready_dv)
                        else:
                            v.wait_ge(sPE.h, pe_need)
                        v.tensor_copy(xb[:], X[:, fc * TL:(fc + 1) * TL]).then_inc(sDV.h, 1)
                    return f
                em("vector", mk_xb(fc, sPE.n))
                dv_at = sDV.bump(1)

                def mk_sq(dv_at):
                    def f(s):
                        s.wait_ge(sDV.h, dv_at)
                        s.activation(xsq[:], xb[:], AF.Square).then_inc(sAC.h, 1)
                    return f
                em("scalar", mk_sq(dv_at))
                ac_at = sAC.bump(1)

                def mk_sum(fc, dv_at, ac_at, prev_free):
                    def f(t):
                        t.wait_ge(sDV.h, dv_at)
                        t.wait_ge(sAC.h, ac_at)
                        if fc == 0 and prev_free:
                            t.wait_ge(sDV.h, prev_free)
                        for half in range(2):
                            t.matmul(pC[0:1, half * 512:half * 512 + 512], ones_col[:],
                                     xb[:, half * 512:half * 512 + 512],
                                     start=(fc == 0), stop=(fc == NF - 1))
                        t.matmul(pD[0:1, :], ones_col[:], xsq[:, 0:512],
                                 start=(fc == 0), stop=(fc == NF - 1))
                        ins = t.matmul(pE[0:1, :], ones_col[:], xsq[:, 512:1024],
                                       start=(fc == 0), stop=(fc == NF - 1))
                        ins.then_inc(sPE.h, 1)
                    return f
                em("tensor", mk_sum(fc, dv_at, ac_at, prev_free))
                sPE.bump(1)
            STATS_PE = sPE.n

            def p_s1(v, ST1a=sDV.n + 1):
                v.wait_ge(sPE.h, STATS_PE)
                v.tensor_scalar(srow1[0:1, :], pC[0:1, 0:TL], 1.0 / D, None, ALU.mult)
                v.tensor_scalar(srow0[0:1, 0:512], pD[0:1, :], 1.0 / D, None, ALU.mult)
                v.tensor_scalar(srow0[0:1, 512:1024], pE[0:1, :], 1.0 / D, None, ALU.mult)
                v.tensor_mul(rrowb[0:1, :], srow1[0:1, :], srow1[0:1, :]).then_inc(sDV.h, 1)
                v.wait_ge(sDV.h, ST1a)
                v.tensor_sub(srow0[0:1, :], srow0[0:1, :], rrowb[0:1, :]).then_inc(sDV.h, 1)
            em("vector", p_s1)
            ST1 = sDV.bump(2)

            def p_s2(s, ST2a=sAC.n + 1):
                s.wait_ge(sDV.h, ST1)
                s.activation(srow0[0:1, :], srow0[0:1, :], AF.Ln, bias=1e-5,
                             scale=1.0).then_inc(sAC.h, 1)
                s.wait_ge(sAC.h, ST2a)
                s.activation(srow0[0:1, :], srow0[0:1, :], AF.Exp, bias=0.0,
                             scale=-0.5).then_inc(sAC.h, 1)
            em("scalar", p_s2)
            ST2 = sAC.bump(2)

            def p_s3(v):
                v.wait_ge(sAC.h, ST2)
                v.scalar_tensor_tensor(xsq[0:1, 0:TL], srow1[0:1, :], -1.0,
                                       srow0[0:1, :], ALU.mult, ALU.mult)
                v.tensor_copy(rrowb[0:1, :], srow0[0:1, :]).then_inc(sDV.h, 1)
            em("vector", p_s3)
            ST3 = sDV.bump(1)

            def p_bc(t):
                t.wait_ge(sDV.h, ST3)
                for half in range(2):
                    t.matmul(pD[:, :] if half == 0 else pE[:, :], ones_row[:],
                             rrowb[0:1, half * 512:half * 512 + 512], start=True, stop=True)
                    ins = t.matmul(pC[:, half * 512:half * 512 + 512], ones_row[:],
                                   xsq[0:1, half * 512:half * 512 + 512],
                                   start=True, stop=True)
                ins.then_inc(sPE.h, 1)
            em("tensor", p_bc)
            BC_PE = sPE.bump(1)

            def p_bcc(v):
                v.wait_ge(sPE.h, BC_PE)
                v.tensor_copy(bcA[:, 0:512], pD[:, :])
                v.tensor_copy(bcA[:, 512:1024], pE[:, :])
                v.tensor_copy(bcB[:], pC[:, 0:TL]).then_inc(sDV.h, 1)
            em("vector", p_bcc)
            BC_DV = sDV.bump(1)
            ln_state["prev_stat_free"] = BC_DV

            for fc2 in range(0, NF, 2):
                def mk_h(fc2):
                    def f(v):
                        if fc2 == 0:
                            v.wait_ge(sDV.h, max(BC_DV, MODV_DONE))
                        v.tensor_mul(xb[:], X[:, fc2 * TL:(fc2 + 1) * TL], bcA[:])
                        v.tensor_mul(xsq[:], X[:, (fc2 + 1) * TL:(fc2 + 2) * TL], bcA[:])
                        v.tensor_add(xb[:], xb[:], bcB[:])
                        v.tensor_add(xsq[:], xsq[:], bcB[:])
                        v.tensor_scalar(out_pan[:, fc2 * TL:(fc2 + 1) * TL], xb[:],
                                        modv[:, ca_col + fc2:ca_col + fc2 + 1],
                                        modv[:, sa_col + fc2:sa_col + fc2 + 1],
                                        ALU.mult, ALU.add)
                        v.tensor_scalar(out_pan[:, (fc2 + 1) * TL:(fc2 + 2) * TL], xsq[:],
                                        modv[:, ca_col + fc2 + 1:ca_col + fc2 + 2],
                                        modv[:, sa_col + fc2 + 1:sa_col + fc2 + 2],
                                        ALU.mult, ALU.add).then_inc(sDV.h, 1)
                    return f
                em("vector", mk_h(fc2))
                sDV.bump(1)
            return sDV.n

        def proj_em(src_pan, wcol, nech, out_cb, h_dv,
                    psum_free=None, wbuf_free_pe=0):
            p_pe = []
            p_dv = []
            for ech in range(nech):
                wb = wec[ech % 2]
                pe_need = p_pe[ech - 2] if ech >= 2 else wbuf_free_pe

                ssw = sW if ech % 2 == 0 else sWb

                def mk_wl(ech, wb, pe_need, ssw=ssw):
                    def f(g):
                        if pe_need:
                            g.wait_ge(sPE.h, pe_need)
                        g.dma_start(out=wb[:], in_=wcol[ech, :, :]).then_inc(ssw.h, 16)
                    return f
                em("gpsimd", mk_wl(ech, wb, pe_need))
                ssw.bump(16)
                w_at = ssw.n
                pp = pA if ech % 2 == 0 else pB
                dv_need = p_dv[ech - 2] if ech >= 2 else None

                def mk_mm(ech, wb, pp, w_at, dv_need, ssw=ssw):
                    def f(t):
                        t.wait_ge(ssw.h, w_at)
                        if ech == 0:
                            t.wait_ge(sDV.h, h_dv)
                        if dv_need is not None:
                            t.wait_ge(sDV.h, dv_need)
                        elif psum_free is not None:
                            t.wait_ge(psum_free[0].h, psum_free[1])
                        for kch in range(NF):
                            for half in range(2):
                                ins = t.matmul(
                                    pp[:, half * 512:half * 512 + 512],
                                    wb[:, kch * 128:(kch + 1) * 128],
                                    src_pan[:, kch * TL + half * 512: kch * TL + half * 512 + 512],
                                    start=(kch == 0), stop=(kch == NF - 1))
                        ins.then_inc(sPE.h, 1)
                    return f
                em("tensor", mk_mm(ech, wb, pp, w_at, dv_need))
                p_pe.append(sPE.bump(1))
                out_cb(ech, pp, p_pe[-1])
                p_dv.append(sDV.n)
            return sPE.n

        ln_state["prev_stat_free"] = max(WM_EVAC, MSC_A)
        H_A_DONE = ln_site(X_READY, 16, 0, hpan)

        if KCUT >= 3:
            qem = qarena

            def q_out(ech, pp, pe_at):
                def f(v):
                    v.wait_ge(sPE.h, pe_at)
                    v.tensor_scalar(qem[:, ech * TL:(ech + 1) * TL], pp[:],
                                    biasq[:, ech:ech + 1], None, ALU.add).then_inc(sDV.h, 1)
                em("vector", f)
                sDV.bump(1)
            QPROJ_PE = proj_em(hpan, l0wq, NF, q_out, H_A_DONE,
                               psum_free=(sDV, X_READY))
            Q_DONE = sDV.n

            kpub = []

            def k_out(ech, pp, pe_at):
                ssk = sKVp if ech % 2 == 0 else sKPb

                def f2(v, ech=ech, pp=pp, pe_at=pe_at, ssk=ssk):
                    v.wait_ge(sPE.h, pe_at)
                    if ech >= 2:
                        v.wait_ge(ssk.h, kpub[ech - 2])
                    v.tensor_scalar(krot[:, (ech % 2) * TL:(ech % 2) * TL + TL], pp[:],
                                    biask[:, ech:ech + 1], None, ALU.add).then_inc(sDV.h, 1)
                em("vector", f2)
                dv_at = sDV.bump(1)

                def g2(g, ech=ech, dv_at=dv_at, ssk=ssk):
                    g.wait_ge(sDV.h, dv_at)
                    g.dma_start(out=ksh[ech * 128:(ech + 1) * 128, :],
                                in_=krot[:, (ech % 2) * TL:(ech % 2) * TL + TL]).then_inc(ssk.h, 16)
                em("gpsimd", g2)
                kpub.append(ssk.bump(16))
            KPROJ_PE = proj_em(hpan, l0wk, NF, k_out, H_A_DONE,
                               psum_free=(sDV, Q_DONE), wbuf_free_pe=QPROJ_PE)
            K_DONE_DV = sDV.n
            sKVp.n_kdone = sKVp.n
            sKPb.n_kdone = sKPb.n

            vtr_pe = []
            vev_dv = []
            vpub = []
            pCb = pC[:].bitcast(BF16)

            def v_out(ech, pp, pe_at):
                def f2(v, ech=ech, pp=pp, pe_at=pe_at):
                    v.wait_ge(sPE.h, pe_at)
                    if ech == 0:
                        v.wait_ge(sKVp.h, sKVp.n_kdone)
                        v.wait_ge(sKPb.h, sKPb.n_kdone)
                    if ech >= 2:
                        v.wait_ge(sPE.h, vtr_pe[ech - 2])
                    v.tensor_scalar(vrot[:, (ech % 2) * TL:(ech % 2) * TL + TL], pp[:],
                                    biasv[:, ech:ech + 1], None, ALU.add).then_inc(sDV.h, 1)
                em("vector", f2)
                dv_at = sDV.bump(1)

                def f3(t, ech=ech, dv_at=dv_at):
                    t.wait_ge(sDV.h, dv_at)
                    if ech >= 1:
                        t.wait_ge(sDV.h, vev_dv[ech - 1])
                    for tb in range(8):
                        ins = t.transpose(
                            pCb[:, tb * 128:(tb + 1) * 128],
                            vrot[:, (ech % 2) * TL + tb * 128:(ech % 2) * TL + tb * 128 + 128],
                            ident[:])
                    ins.then_inc(sPE.h, 1)
                em("tensor", f3)
                vtr_pe.append(sPE.bump(1))

                def f4(v, ech=ech, pe_at2=vtr_pe[-1]):
                    v.wait_ge(sPE.h, pe_at2)
                    if ech == 0:
                        v.wait_ge(sDV.h, MEAN_RED)
                    if ech >= 1:
                        v.wait_ge(sKVp.h, vpub[ech - 1])
                    for tb in range(8):
                        ins = v.tensor_copy(vtout[:, tb * 128:(tb + 1) * 128],
                                            pCb[:, tb * 128:(tb + 1) * 128])
                    ins.then_inc(sDV.h, 1)
                em("vector", f4)
                vev_dv.append(sDV.bump(1))

                def f5(g, ech=ech, dv_at2=vev_dv[-1]):
                    g.wait_ge(sDV.h, dv_at2)
                    for tb in range(8):
                        g.dma_start(out=vsh[(ech * 8 + tb) * 128:(ech * 8 + tb + 1) * 128, :],
                                    in_=vtout[:, tb * 128:(tb + 1) * 128]).then_inc(sKVp.h, 16)
                em("gpsimd", f5)
                vpub.append(sKVp.bump(16 * 8))
            VPROJ_PE = proj_em(hpan, l0wv, NF, v_out, H_A_DONE,
                               psum_free=(sDV, K_DONE_DV), wbuf_free_pe=KPROJ_PE)
            V_PUB = sKVp.n
            V_PUB_DV = sDV.n

            def p_ag_kv(g):
                g.wait_ge(sKVp.h, V_PUB)
                g.wait_ge(sKPb.h, sKPb.n_kdone)
                g.collective_compute("AllGather", ALU.bypass, replica_groups=RG,
                                     ins=[ksh.ap()], outs=[kfull.ap()]).then_inc(sKV.h, 1)
                g.collective_compute("AllGather", ALU.bypass, replica_groups=RG,
                                     ins=[vsh.ap()], outs=[vfull.ap()]).then_inc(sKV.h, 1)
            em("gpsimd", p_ag_kv)
            KV_AG = sKV.bump(2)

        if KCUT >= 4:
            avn = hpan
            kbuf = scrb[:, 0:NI0]
            vbuf = scrb[:, NI0:2 * NI0]
            fl_state = {"prev_norm_dv": 0, "prev_last_exp": 0, "v_consumed": 0,
                        "prev_sc_last": 0}

            for hh in range(H):
                def mk_kl(hh, sc_last=fl_state["prev_sc_last"]):
                    def f(e):
                        if hh == 0:
                            e.wait_ge(sKV.h, KV_AG)
                        else:
                            e.wait_ge(sPE.h, sc_last)
                        for r in range(NCORES):
                            e.dma_start(out=kbuf[:, r * 1024:(r + 1) * 1024],
                                        in_=kfull[r, hh * 128:(hh + 1) * 128, :]).then_inc(sFL.h, 16)
                    return f
                em("sync", mk_kl(hh))
                K_LD0 = sFL.n
                K_LD = sFL.bump(16 * 8)

                def mk_vl(hh, v_consumed):
                    def f(e):
                        if hh == 0:
                            e.wait_ge(sKV.h, KV_AG)
                        else:
                            e.wait_ge(sPE.h, v_consumed)
                        for r in range(NCORES):
                            e.dma_start(
                                out=vbuf[:, r * 1024:(r + 1) * 1024]
                                .rearrange("p (a c) -> p a c", c=128),
                                in_=vfull[r, hh * 1024:(hh + 1) * 1024, :]
                                .rearrange("(a b) c -> b a c", b=128)).then_inc(sFLV.h, 16)
                    return f
                em("sync", mk_vl(hh, fl_state["v_consumed"]))
                V_LD0 = sFLV.n
                V_LD = sFLV.bump(16 * 8)

                kb = kbuf
                pe0 = sPE.n
                ac0 = sAC.n
                dv0 = sDV.n
                prev_norm = fl_state["prev_norm_dv"]
                prev_exp = fl_state["prev_last_exp"]

                def pe_scores(k, pe0=pe0):
                    return pe0 + (k + 1 if k <= 1 else 2 * k)

                def pe_av(k, pe0=pe0):
                    return pe0 + 2 * k + 3 if k < 63 else pe0 + 128

                seq = [("s", 0), ("s", 1)]
                for k in range(2, 64):
                    seq.append(("av", k - 2))
                    seq.append(("s", k))
                seq.append(("av", 62))
                seq.append(("av", 63))

                def mk_s(hh, k, kb, K_LD, prev_exp, ac0):
                    pp = pA if k % 2 == 0 else pB

                    def f(t):
                        if k == 0:
                            t.wait_ge(sFL.h, K_LD)
                        if k >= 2:
                            t.wait_ge(sAC.h, ac0 + k - 1)
                        elif prev_exp:
                            t.wait_ge(sAC.h, prev_exp)
                        for half in range(2):
                            ins = t.matmul(pp[:, half * 512:half * 512 + 512],
                                           kb[:, k * 128:(k + 1) * 128],
                                           qem[:, hh * TL + half * 512: hh * TL + half * 512 + 512],
                                           start=True, stop=True)
                        ins.then_inc(sPE.h, 1)
                    return f

                def mk_avk(hh, k, V_LD, prev_norm, ac0):
                    def f(t):
                        t.wait_ge(sAC.h, ac0 + k + 1)
                        if k == 0:
                            t.wait_ge(sFLV.h, V_LD)
                            if prev_norm:
                                t.wait_ge(sDV.h, prev_norm)
                        for half in range(2):
                            ins = t.matmul(pC[:, half * 512:half * 512 + 512],
                                           vbuf[:, k * 128:(k + 1) * 128],
                                           pbuf[k % 2][:, half * 512:half * 512 + 512],
                                           start=(k == 0), stop=(k == 63))
                        ins.then_inc(sPE.h, 1)
                    return f
                for kind, k in seq:
                    if kind == "s":
                        em("tensor", mk_s(hh, k, kb, K_LD, prev_exp, ac0))
                    else:
                        em("tensor", mk_avk(hh, k, V_LD, prev_norm, ac0))
                    sPE.bump(1)

                for k in range(64):
                    def mk_e(hh, k, pe_s, pe_a, dv0):
                        pp = pA if k % 2 == 0 else pB

                        def f(s):
                            s.wait_ge(sPE.h, pe_s)
                            if hh == 0 and k == 0:
                                s.wait_ge(sDV.h, V_PUB_DV)
                            if k >= 2:
                                s.wait_ge(sPE.h, pe_a)
                                s.wait_ge(sDV.h, dv0 + k - 1)
                            s.activation(pbuf[k % 2][:], pp[:], AF.Exp,
                                         bias=EXPSHIFT, scale=INVSQ).then_inc(sAC.h, 1)
                        return f
                    em("scalar", mk_e(hh, k, pe_scores(k), pe_av(k - 2) if k >= 2 else 0, dv0))
                    sAC.bump(1)

                    def mk_d(hh, k, ac_at):
                        def f(v):
                            v.wait_ge(sAC.h, ac_at)
                            if k == 0:
                                v.tensor_copy(den[:], pbuf[0][:]).then_inc(sDV.h, 1)
                            else:
                                v.tensor_add(den[:, 0:512], den[:, 0:512],
                                             pbuf[k % 2][:, 0:512])
                                v.tensor_add(den[:, 512:1024], den[:, 512:1024],
                                             pbuf[k % 2][:, 512:1024]).then_inc(sDV.h, 1)
                        return f
                    em("vector", mk_d(hh, k, ac0 + k + 1))
                    sDV.bump(1)
                DEN_DV = sDV.n
                LAST_EXP = sAC.n
                AV_PE = sPE.n

                def mk_ds(hh, DEN_DV=DEN_DV):
                    def f(t):
                        t.wait_ge(sDV.h, DEN_DV)
                        t.matmul(pD[0:1, :], ones_col[:], den[:, 0:512], start=True, stop=True)
                        t.matmul(pE[0:1, :], ones_col[:], den[:, 512:1024],
                                 start=True, stop=True).then_inc(sPE.h, 1)
                    return f
                em("tensor", mk_ds(hh))
                DS_PE = sPE.bump(1)

                def mk_rc1(hh, DS_PE=DS_PE):
                    def f(v):
                        v.wait_ge(sPE.h, DS_PE)
                        v.tensor_copy(srow0[0:1, 0:512], pD[0:1, :])
                        v.tensor_copy(srow0[0:1, 512:1024], pE[0:1, :]).then_inc(sDV.h, 1)
                    return f
                em("vector", mk_rc1(hh))
                RC1 = sDV.bump(1)

                def mk_rc2(hh, RC1=RC1):
                    def f(v):
                        v.wait_ge(sDV.h, RC1)
                        v.reciprocal(srow1[0:1, :], srow0[0:1, :]).then_inc(sDV.h, 1)
                        v.wait_ge(sDV.h, RC1 + 1)
                        v.tensor_copy(rrowb[0:1, :], srow1[0:1, :]).then_inc(sDV.h, 1)
                    return f
                em("vector", mk_rc2(hh))
                RC2 = sDV.bump(2)

                def mk_bcd(hh, RC2=RC2):
                    def f(t):
                        t.wait_ge(sDV.h, RC2)
                        t.matmul(pD[:, :], ones_row[:], rrowb[0:1, 0:512], start=True, stop=True)
                        t.matmul(pE[:, :], ones_row[:], rrowb[0:1, 512:1024],
                                 start=True, stop=True).then_inc(sPE.h, 1)
                    return f
                em("tensor", mk_bcd(hh))
                BCD_PE = sPE.bump(1)

                def mk_nrm(hh, BCD_PE=BCD_PE):
                    def f(v):
                        v.wait_ge(sPE.h, BCD_PE)
                        v.tensor_copy(bcD[:, 0:512], pD[:, :])
                        v.tensor_copy(bcD[:, 512:1024], pE[:, :])
                        v.tensor_mul(avn[:, hh * TL:hh * TL + 512], pC[:, 0:512],
                                     bcD[:, 0:512])
                        v.tensor_mul(avn[:, hh * TL + 512:(hh + 1) * TL], pC[:, 512:TL],
                                     bcD[:, 512:1024]).then_inc(sDV.h, 1)
                    return f
                em("vector", mk_nrm(hh))
                NRM_DV = sDV.bump(1)
                fl_state["prev_norm_dv"] = NRM_DV
                fl_state["prev_last_exp"] = LAST_EXP
                fl_state["v_consumed"] = AV_PE
                fl_state["prev_sc_last"] = pe_scores(63)
            AVN_DONE = sDV.n
            FL_LAST_EXP = fl_state["prev_last_exp"]

        if KCUT >= 5:
            def p_gabo(v):
                v.wait_ge(sDV.h, MODV_DONE)
                v.wait_ge(sConst.h, CONST_DONE)
                v.tensor_mul(modv[:, 256:272], modv[:, 32:48], biaso[:]).then_inc(sDV.h, 1)
            em("vector", p_gabo)
            GABO_DONE = sDV.bump(1)

            def o_out(ech, pp, pe_at):
                def f(v, ech=ech, pp=pp, pe_at=pe_at):
                    v.wait_ge(sPE.h, pe_at)
                    if ech == 0:
                        v.wait_ge(sDV.h, GABO_DONE)
                    v.scalar_tensor_tensor(X[:, ech * TL:(ech + 1) * TL], pp[:],
                                           modv[:, 32 + ech:32 + ech + 1],
                                           X[:, ech * TL:(ech + 1) * TL],
                                           ALU.mult, ALU.add).then_inc(sDV.h, 1)
                em("vector", f)
                sDV.bump(1)
                if ech >= 1:
                    def f2(v, e2=ech - 1):
                        v.tensor_scalar(X[:, e2 * TL:(e2 + 1) * TL],
                                        X[:, e2 * TL:(e2 + 1) * TL],
                                        modv[:, 256 + e2:256 + e2 + 1], None,
                                        ALU.add).then_inc(sDV.h, 1)
                    em("vector", f2)
                    sDV.bump(1)
            proj_em(avn, l0wo, NF, o_out, AVN_DONE,
                    psum_free=(sAC, FL_LAST_EXP), wbuf_free_pe=VPROJ_PE)

            def o_flush(v):
                v.tensor_scalar(X[:, 15 * TL:16 * TL], X[:, 15 * TL:16 * TL],
                                modv[:, 256 + 15:256 + 16], None, ALU.add).then_inc(sDV.h, 1)
            em("vector", o_flush)
            sDV.bump(1)
            X1_DONE = sDV.n
        else:
            X1_DONE = X_READY

        if KCUT >= 6:
            sQL = sem("sQL"); sQP = sem("sQP"); sSP = sem("sSP")
            sIQ = sem("sIQ"); sFQ = sem("sFQ"); sDT = sem("sDT"); sDTb = sem("sDTb")
            pbank = [pA[:, 0:512], pA[:, 512:1024], pB[:, 0:512], pB[:, 512:1024],
                     pC[:, 0:512], pC[:, 512:1024], pD[:, :], pE[:, :]]

            qspecs = []
            for r in range(2):
                for cc in range(8):
                    qspecs.append((0, l0wg_s[r * 128:(r + 1) * 128, cc * 1024:(cc + 1) * 1024],
                                   fqg_sh[cc * 8:(cc + 1) * 8, r, :, :], 3, 0))
            for r in range(2):
                for cc in range(8):
                    qspecs.append((1, l0wu_s[r * 128:(r + 1) * 128, cc * 1024:(cc + 1) * 1024],
                                   fqu_sh[cc * 8:(cc + 1) * 8, r, :, :], 4, 1))
            for r in range(8):
                for cc in range(2):
                    qspecs.append((2, l0wd_s[r * 128:(r + 1) * 128, cc * 1024:(cc + 1) * 1024],
                                   fqd_sh[r, cc, :, :, :], 5, 2))
            q_dv = []
            q_qp = []
            for qi, (j, srcap, dstap, invcol, mcol) in enumerate(qspecs):
                def mk_qload(qi, srcap):
                    def f(e):
                        if qi >= 1:
                            e.wait_ge(sDV.h, q_dv[qi - 1])
                        e.dma_start(out=qin[:], in_=srcap).then_inc(sQL.h, 16)
                    return f
                em("sync", mk_qload(qi, srcap))
                ql_at = sQL.bump(16)

                def mk_qdve(qi, ql_at, invcol, mcol, B=sDV.n):
                    def f(v):
                        v.wait_ge(sQL.h, ql_at)
                        if qi == 0:
                            v.wait_ge(sConst.h, SCL_DONE)
                        if qi >= 1:
                            v.wait_ge(sQP.h, q_qp[qi - 1])
                        v.tensor_scalar(qin[:], qin[:], scl[:, invcol:invcol + 1],
                                        MAGIC, ALU.mult, ALU.add).then_inc(sDV.h, 1)
                        v.wait_ge(sDV.h, B + 1)
                        v.tensor_scalar(qin[:], qin[:], MAGIC, 1.0, ALU.subtract,
                                        ALU.min).then_inc(sDV.h, 1)
                        v.wait_ge(sDV.h, B + 2)
                        v.tensor_scalar(qstage[:], qin[:], -1.0, scl[:, mcol:mcol + 1],
                                        ALU.max, ALU.mult).then_inc(sDV.h, 1)
                    return f
                em("vector", mk_qdve(qi, ql_at, invcol, mcol))
                q_dv.append(sDV.bump(3))

                def mk_qpub(qi, dstap):
                    def f(g):
                        g.wait_ge(sDV.h, q_dv[qi])
                        g.dma_start(out=dstap.rearrange("a p c -> p a c"),
                                    in_=qstage[:].rearrange("p (a c) -> p a c", c=128)
                                    ).then_inc(sQP.h, 16)
                    return f
                em("gpsimd", mk_qpub(qi, dstap))
                q_qp.append(sQP.bump(16))
            QPUB_ALL = sQP.n

            def p_ag_fq(g):
                g.wait_ge(sQP.h, QPUB_ALL)
                g.collective_compute("AllGather", ALU.bypass, replica_groups=RG,
                                     ins=[fqg_sh.ap()], outs=[fqg_full.ap()]).then_inc(sKV.h, 1)
                g.collective_compute("AllGather", ALU.bypass, replica_groups=RG,
                                     ins=[fqu_sh.ap()], outs=[fqu_full.ap()]).then_inc(sKV.h, 1)
                g.collective_compute("AllGather", ALU.bypass, replica_groups=RG,
                                     ins=[fqd_sh.ap()], outs=[fqd_full.ap()]).then_inc(sKV.h, 1)
            em("gpsimd", p_ag_fq)
            FQ_AG = sKV.bump(3)

            ln_state["prev_stat_free"] = fl_state["prev_norm_dv"]
            H_B_DONE = ln_site(X1_DONE, 64, 48, hpan)

            def p_amax(v, B=sDV.n):
                v.wait_ge(sDV.h, H_B_DONE)
                v.tensor_tensor(xsq[:], hpan[:, 0:TL], hpan[:, TL:2 * TL], ALU.min)
                v.tensor_tensor(xb[:], hpan[:, 0:TL], hpan[:, TL:2 * TL], ALU.max)
                for i in range(2, NF):
                    v.tensor_tensor(xsq[:], xsq[:], hpan[:, i * TL:(i + 1) * TL], ALU.min)
                    v.tensor_tensor(xb[:], xb[:], hpan[:, i * TL:(i + 1) * TL], ALU.max)
                v.tensor_scalar(xsq[:], xsq[:], -1.0, None, ALU.mult).then_inc(sDV.h, 1)
                v.wait_ge(sDV.h, B + 1)
                v.tensor_tensor(xb[:], xb[:], xsq[:], ALU.max).then_inc(sDV.h, 1)
            em("vector", p_amax)
            AMAX_DV = sDV.bump(2)

            def p_amtr(t):
                t.wait_ge(sDV.h, AMAX_DV)
                for b in range(8):
                    ins = t.transpose(pCb[:, b * 128:(b + 1) * 128],
                                      xb[:, b * 128:(b + 1) * 128], ident[:])
                ins.then_inc(sPE.h, 1)
            em("tensor", p_amtr)
            AMTR_PE = sPE.bump(1)

            def p_amred(v):
                v.wait_ge(sPE.h, AMTR_PE)
                for b in range(8):
                    v.tensor_reduce(mred[:, 16 + b:17 + b], pCb[:, b * 128:(b + 1) * 128],
                                    AX, ALU.max).then_inc(sDV.h, 1)
            em("vector", p_amred)
            AMRED_DV = sDV.bump(8)

            def p_ascale(v, W0=sDV.n + 1, W1=sDV.n + 2):
                v.wait_ge(sDV.h, AMRED_DV)
                v.tensor_scalar(mred[:, 16:24], mred[:, 16:24], 1e-5, None,
                                ALU.max).then_inc(sDV.h, 1)
                v.wait_ge(sDV.h, W0)
                v.reciprocal(mred[:, 0:8], mred[:, 16:24]).then_inc(sDV.h, 1)
                v.wait_ge(sDV.h, W1)
                v.tensor_scalar(mred[:, 0:8], mred[:, 0:8], 128.0, None, ALU.mult)
                v.tensor_scalar(mred[:, 8:16], mred[:, 16:24], 1.0 / 128.0, None,
                                ALU.mult).then_inc(sDV.h, 1)
            em("vector", p_ascale)
            ASC_DV = sDV.bump(3)

            def p_astr(t, W=sDV.n + 1):
                t.wait_ge(sDV.h, ASC_DV)
                for c in range(8):
                    pp = pD if c < 4 else pE
                    ins = t.transpose(pp[0:1, (c % 4) * 128:(c % 4) * 128 + 128],
                                      mred[:, c:c + 1], identf[:])
                ins.then_inc(sPE.h, 1)
                t.wait_ge(sDV.h, W)
                for c in range(8):
                    pp = pD if c < 4 else pE
                    ins = t.transpose(pp[0:1, (c % 4) * 128:(c % 4) * 128 + 128],
                                      mred[:, 8 + c:9 + c], identf[:])
                ins.then_inc(sPE.h, 1)
            em("tensor", p_astr)
            ASTR_PE = sPE.bump(2)

            qinb = qin[:].bitcast(BF16)
            def p_asev(v):
                v.wait_ge(sPE.h, ASTR_PE - 1)
                v.tensor_copy(qinb[0:1, 0:512], pD[0:1, 0:512])
                v.tensor_copy(qinb[0:1, 512:1024], pE[0:1, 0:512]).then_inc(sDV.h, 1)
                v.wait_ge(sPE.h, ASTR_PE)
                v.tensor_copy(qinb[0:1, 1024:1536], pD[0:1, 0:512])
                v.tensor_copy(qinb[0:1, 1536:2048], pE[0:1, 0:512]).then_inc(sDV.h, 1)
            em("vector", p_asev)
            ASEV_DV = sDV.bump(2)

            def p_abc(t):
                t.wait_ge(sDV.h, ASEV_DV)
                for b in range(8):
                    t.matmul(pC[:, b * 128:(b + 1) * 128], ones_row[:],
                             qinb[0:1, b * 128:(b + 1) * 128], start=True, stop=True)
                for b in range(8):
                    ins = t.matmul(pB[:, b * 128:(b + 1) * 128], ones_row[:],
                                   qinb[0:1, 1024 + b * 128:1024 + (b + 1) * 128],
                                   start=True, stop=True)
                ins.then_inc(sPE.h, 1)
            em("tensor", p_abc)
            ABC_PE = sPE.bump(1)

            def p_abcev(v):
                v.wait_ge(sPE.h, ABC_PE)
                v.tensor_copy(shared4k[:, 0:TL], pC[:, 0:TL])
                v.tensor_copy(shared4k[:, TL:2 * TL], pB[:, 0:TL]).then_inc(sDV.h, 1)
            em("vector", p_abcev)
            ABCEV_DV = sDV.bump(1)
            bcS = shared4k[:, 0:TL]
            bcI = shared4k[:, TL:2 * TL]
            big8f = big8[:]

            aq_dv = []
            for fc2 in range(0, NF, 2):
                def mk_aq(fc2):
                    def f(v):
                        if fc2 == 0:
                            v.wait_ge(sDV.h, ABCEV_DV)
                        v.tensor_tensor(big8f[:, 0:TL], hpan[:, fc2 * TL:(fc2 + 1) * TL], bcS, ALU.mult)
                        v.tensor_tensor(big8f[:, TL:2 * TL], hpan[:, (fc2 + 1) * TL:(fc2 + 2) * TL], bcS, ALU.mult)
                        v.tensor_scalar(big8f[:, 0:TL], big8f[:, 0:TL], MAGIC, MAGIC, ALU.add, ALU.subtract)
                        v.tensor_scalar(big8f[:, TL:2 * TL], big8f[:, TL:2 * TL], MAGIC, MAGIC, ALU.add, ALU.subtract)
                        v.tensor_scalar(big8f[:, 0:TL], big8f[:, 0:TL], 127.0, -128.0, ALU.min, ALU.max)
                        v.tensor_scalar(big8f[:, TL:2 * TL], big8f[:, TL:2 * TL], 127.0, -128.0, ALU.min, ALU.max)
                        v.tensor_tensor(hpan[:, fc2 * TL:(fc2 + 1) * TL], big8f[:, 0:TL], bcI, ALU.mult)
                        v.tensor_tensor(hpan[:, (fc2 + 1) * TL:(fc2 + 2) * TL],
                                        big8f[:, TL:2 * TL], bcI, ALU.mult).then_inc(sDV.h, 1)
                    return f
                em("vector", mk_aq(fc2))
                aq_dv.append(sDV.bump(1))
            AQ_DONE = sDV.n
            aq = hpan

            if DBG6:
                def p_dbg6a(g):
                    g.wait_ge(sDV.h, AQ_DONE)
                    g.dma_start(out=t_aq, in_=aq[:]).then_inc(sST.h, 16)
                    g.wait_ge(sConst.h, SCL_DONE)
                    g.dma_start(out=t_scl, in_=scl[:]).then_inc(sST.h, 16)
                    g.wait_ge(sDV.h, H128_DONE)
                    g.dma_start(out=t_h128, in_=h128[:]).then_inc(sST.h, 16)
                    g.wait_ge(sDV.h, H_B_DONE)
                    g.dma_start(out=t_hb, in_=hpan[:]).then_inc(sST.h, 16)
                em("gpsimd", p_dbg6a)
                sST.bump(64)
            gu_pe = []
            gu_sp = []
            gu_dvi = []
            gu_ac = []
            for nch in range(64):
                def mk_gul(nch):
                    def f(e):
                        if nch == 0:
                            e.wait_ge(sKV.h, FQ_AG)
                        if nch >= 1:
                            e.wait_ge(sPE.h, gu_pe[nch - 1])
                        for r in range(NCORES):
                            e.dma_start(out=wec[0][:, r * 256:(r + 1) * 256]
                                        .rearrange("p (a c) -> p a c", c=128),
                                        in_=fqg_full[r, nch, :, :, :]
                                        .rearrange("a p c -> p a c")).then_inc(sFQ.h, 16)
                        for r in range(NCORES):
                            e.dma_start(out=wec[1][:, r * 256:(r + 1) * 256]
                                        .rearrange("p (a c) -> p a c", c=128),
                                        in_=fqu_full[r, nch, :, :, :]
                                        .rearrange("a p c -> p a c")).then_inc(sFQ.h, 16)
                    return f
                em("scalar", mk_gul(nch))
                FQ_AT = sFQ.bump(16 * 16)

                def mk_gumm(nch, FQ_AT):
                    def f(t):
                        t.wait_ge(sFQ.h, FQ_AT)
                        if nch == 0:
                            t.wait_ge(sDV.h, AQ_DONE)
                        if nch >= 1:
                            t.wait_ge(sDV.h, gu_dvi[nch - 1])
                        for kch in range(NF):
                            for half in range(2):
                                t.matmul(pA[:, half * 512:half * 512 + 512],
                                         wec[0][:, kch * 128:(kch + 1) * 128],
                                         aq[:, kch * TL + half * 512: kch * TL + half * 512 + 512],
                                         start=(kch == 0), stop=(kch == NF - 1))
                                ins = t.matmul(pB[:, half * 512:half * 512 + 512],
                                               wec[1][:, kch * 128:(kch + 1) * 128],
                                               aq[:, kch * TL + half * 512: kch * TL + half * 512 + 512],
                                               start=(kch == 0), stop=(kch == NF - 1))
                        ins.then_inc(sPE.h, 1)
                    return f
                em("tensor", mk_gumm(nch, FQ_AT))
                gu_pe.append(sPE.bump(1))

                def mk_gusilu(nch, pe_at):
                    def f(s):
                        s.wait_ge(sPE.h, pe_at)
                        if nch >= 1:
                            s.wait_ge(sDV.h, gu_dvi[nch - 1])
                        s.activation(xb[:], pA[:, 0:TL], AF.Silu).then_inc(sAC.h, 1)
                    return f
                em("scalar", mk_gusilu(nch, gu_pe[-1]))
                gu_ac.append(sAC.bump(1))

                def mk_guev(nch, ac_at):
                    def f(v):
                        v.wait_ge(sAC.h, ac_at)
                        if nch >= 1:
                            v.wait_ge(sSP.h, gu_sp[nch - 1])
                        v.tensor_tensor(xsq[:], xb[:], pB[:, 0:TL], ALU.mult).then_inc(sDV.h, 1)
                    return f
                em("vector", mk_guev(nch, gu_ac[-1]))
                gu_dvi.append(sDV.bump(1))

                def mk_gusp(nch, dv_at):
                    def f(e):
                        e.wait_ge(sDV.h, dv_at)
                        e.dma_start(out=inter_dram[nch, :, :], in_=xsq[:]).then_inc(sSP.h, 16)
                    return f
                em("sync", mk_gusp(nch, gu_dvi[-1]))
                gu_sp.append(sSP.bump(16))
            GU_SPILL = sSP.n
            if DBG6:
                def p_dbg6b(g):
                    g.wait_ge(sSP.h, GU_SPILL)
                    g.dma_start(out=t_int.rearrange("a p t -> (a p) t"),
                                in_=inter_dram[:, :, :].rearrange("a p t -> (a p) t")).then_inc(sST.h, 16)
                em("gpsimd", p_dbg6b)
                sST.bump(16)

            DK = 1.0 / float(np.sqrt(8192.0))
            x2_dv = []
            q_state = {"prev_fq_pe": 0, "prev_x2": 0, "prev_dtl_pe": 0}
            for q in range(4):
                def mk_iql(q, prev_fq_pe):
                    def f(e):
                        e.wait_ge(sSP.h, GU_SPILL)
                        if prev_fq_pe:
                            e.wait_ge(sPE.h, prev_fq_pe)
                        for nch in range(64):
                            e.dma_start(out=scrb[:, nch * 256:(nch + 1) * 256],
                                        in_=inter_dram[nch, :, q * 256:(q + 1) * 256]).then_inc(sIQ.h, 16)
                    return f
                em("sync", mk_iql(q, q_state["prev_fq_pe"]))
                IQ0 = sIQ.n
                IQ_AT = sIQ.bump(16 * 64)

                h_pe = []
                h_dv = []
                for nch in range(64):
                    def mk_h128(q, nch, IQ_AT, px2=q_state["prev_x2"], hdv=h_dv):
                        def f(t):
                            t.wait_ge(sIQ.h, IQ_AT)
                            if nch == 0:
                                t.wait_ge(sDV.h, H128_DONE)
                                t.wait_ge(sDV.h, px2)
                            if nch >= 2:
                                t.wait_ge(sDV.h, hdv[nch - 2])
                            dst = pD[:, 0:256] if nch % 2 == 0 else pE[:, 0:256]
                            t.matmul(dst, h128[:], scrb[:, nch * 256:(nch + 1) * 256],
                                     start=True, stop=True).then_inc(sPE.h, 1)
                        return f
                    em("tensor", mk_h128(q, nch, IQ_AT))
                    h_pe.append(sPE.bump(1))

                    def mk_h128ev(q, nch, pe_at):
                        def f(v):
                            v.wait_ge(sPE.h, pe_at)
                            srcp = pD[:, 0:256] if nch % 2 == 0 else pE[:, 0:256]
                            v.tensor_copy(qarena[:, nch * 256:(nch + 1) * 256],
                                          srcp).then_inc(sDV.h, 1)
                        return f
                    em("vector", mk_h128ev(q, nch, h_pe[-1]))
                    h_dv.append(sDV.bump(1))
                H128_EVAC = sDV.n

                bufs = [qarena, scrb]
                cur = 0
                for st in range(6):
                    d = 1 << st
                    sin_b = bufs[cur]
                    sout_b = bufs[1 - cur]

                    def mk_stage(q, st, d, sin_b, sout_b, first, HEV=None):
                        def f(v):
                            if first:
                                v.wait_ge(sDV.h, HEV)
                            for i0 in range(0, 64, 2 * d):
                                for jj in range(d):
                                    a = (i0 + jj) * 256
                                    b = (i0 + jj + d) * 256
                                    v.tensor_tensor(sout_b[:, a:a + 256], sin_b[:, a:a + 256],
                                                    sin_b[:, b:b + 256], ALU.add)
                                    ins = v.tensor_tensor(sout_b[:, b:b + 256], sin_b[:, a:a + 256],
                                                          sin_b[:, b:b + 256], ALU.subtract)
                            ins.then_inc(sDV.h, 1)
                        return f
                    em("vector", mk_stage(q, st, d, sin_b, sout_b, st == 0, H128_EVAC))
                    sDV.bump(1)
                    cur = 1 - cur
                FW_DONE = sDV.n
                fwb = bufs[cur]
                fqb = bufs[1 - cur]

                def p_am2(q, FW_DONE=None, fwb=None, B=None):
                    B = sDV.n if B is None else B

                    def f(v):
                        v.wait_ge(sDV.h, FW_DONE)
                        v.tensor_tensor(xsq[:, 0:256], fwb[:, 0:256], fwb[:, 256:512], ALU.min)
                        v.tensor_tensor(xb[:, 0:256], fwb[:, 0:256], fwb[:, 256:512], ALU.max)
                        for i in range(2, 64):
                            v.tensor_tensor(xsq[:, 0:256], xsq[:, 0:256],
                                            fwb[:, i * 256:(i + 1) * 256], ALU.min)
                            v.tensor_tensor(xb[:, 0:256], xb[:, 0:256],
                                            fwb[:, i * 256:(i + 1) * 256], ALU.max)
                        v.tensor_scalar(xsq[:, 0:256], xsq[:, 0:256], -1.0, None,
                                        ALU.mult).then_inc(sDV.h, 1)
                        v.wait_ge(sDV.h, B + 1)
                        v.tensor_tensor(xb[:, 0:256], xb[:, 0:256], xsq[:, 0:256],
                                        ALU.max).then_inc(sDV.h, 1)
                    return f
                em("vector", p_am2(q, FW_DONE, fwb))
                AM2_DV = sDV.bump(2)

                def p_am2tr(q, AM2_DV):
                    def f(t):
                        t.wait_ge(sDV.h, AM2_DV)
                        pDbw = pD[:].bitcast(BF16)
                        t.transpose(pDbw[:, 0:128], xb[:, 0:128], ident[:])
                        t.transpose(pDbw[:, 128:256], xb[:, 128:256], ident[:]).then_inc(sPE.h, 1)
                    return f
                em("tensor", p_am2tr(q, AM2_DV))
                AM2TR_PE = sPE.bump(1)

                def p_am2red(q, AM2TR_PE, W0=sDV.n + 2, W1=sDV.n + 3):
                    def f(v):
                        v.wait_ge(sPE.h, AM2TR_PE)
                        pDb = pD[:].bitcast(BF16)
                        v.tensor_reduce(mred[:, 16:17], pDb[:, 0:128], AX, ALU.max)
                        v.tensor_reduce(mred[:, 17:18], pDb[:, 128:256], AX, ALU.max).then_inc(sDV.h, 2)
                        v.wait_ge(sDV.h, W0)
                        v.tensor_scalar(mred[:, 16:18], mred[:, 16:18], 1e-5 / DK, None,
                                        ALU.max).then_inc(sDV.h, 1)
                        v.wait_ge(sDV.h, W1)
                        v.reciprocal(mred[:, 0:2], mred[:, 16:18]).then_inc(sDV.h, 1)
                        v.wait_ge(sDV.h, W1 + 1)
                        v.tensor_scalar(mred[:, 0:2], mred[:, 0:2], 128.0, None, ALU.mult)
                        v.tensor_scalar(mred[:, 2:4], mred[:, 16:18], DK / 128.0, None,
                                        ALU.mult).then_inc(sDV.h, 1)
                    return f
                em("vector", p_am2red(q, AM2TR_PE))
                AM2RED_DV = sDV.bump(5)

                def p_am2str(q, AM2RED_DV):
                    def f(t):
                        t.wait_ge(sDV.h, AM2RED_DV)
                        for c in range(4):
                            ins = t.transpose(pD[0:1, c * 128:(c + 1) * 128],
                                              mred[:, c:c + 1], identf[:])
                        ins.then_inc(sPE.h, 1)
                    return f
                em("tensor", p_am2str(q, AM2RED_DV))
                AM2STR_PE = sPE.bump(1)

                def p_am2ev(q, AM2STR_PE):
                    def f(v):
                        v.wait_ge(sPE.h, AM2STR_PE)
                        v.tensor_copy(qinb[0:1, 0:512], pD[0:1, 0:512]).then_inc(sDV.h, 1)
                    return f
                em("vector", p_am2ev(q, AM2STR_PE))
                AM2EV_DV = sDV.bump(1)

                def p_am2bc(q, AM2EV_DV):
                    def f(t):
                        t.wait_ge(sDV.h, AM2EV_DV)
                        t.matmul(pD[:, 0:128], ones_row[:], qinb[0:1, 0:128], start=True, stop=True)
                        t.matmul(pD[:, 128:256], ones_row[:], qinb[0:1, 128:256], start=True, stop=True)
                        t.matmul(pE[:, 0:128], ones_row[:], qinb[0:1, 256:384], start=True, stop=True)
                        t.matmul(pE[:, 128:256], ones_row[:], qinb[0:1, 384:512],
                                 start=True, stop=True).then_inc(sPE.h, 1)
                    return f
                em("tensor", p_am2bc(q, AM2EV_DV))
                AM2BC_PE = sPE.bump(1)

                def p_am2bcev(q, AM2BC_PE):
                    def f(v):
                        v.wait_ge(sPE.h, AM2BC_PE)
                        v.tensor_copy(shared4k[:, 0:256], pD[:, 0:256])
                        v.tensor_copy(shared4k[:, 256:512], pE[:, 0:256]).then_inc(sDV.h, 1)
                    return f
                em("vector", p_am2bcev(q, AM2BC_PE))
                AM2BCEV_DV = sDV.bump(1)
                bcS2 = shared4k[:, 0:256]
                bcI2 = shared4k[:, 256:512]

                fq_dv = []
                for n2 in range(0, 64, 2):
                    def mk_fq(q, n2, AM2BCEV_DV, fwb=fwb, fqb=fqb):
                        def f(v):
                            if n2 == 0:
                                v.wait_ge(sDV.h, AM2BCEV_DV)
                            a = n2 * 256
                            b = (n2 + 1) * 256
                            v.tensor_tensor(big8f[:, 0:256], fwb[:, a:a + 256], bcS2, ALU.mult)
                            v.tensor_tensor(big8f[:, 256:512], fwb[:, b:b + 256], bcS2, ALU.mult)
                            v.tensor_scalar(big8f[:, 0:256], big8f[:, 0:256], MAGIC, MAGIC, ALU.add, ALU.subtract)
                            v.tensor_scalar(big8f[:, 256:512], big8f[:, 256:512], MAGIC, MAGIC, ALU.add, ALU.subtract)
                            v.tensor_scalar(fqb[:, a:a + 256], big8f[:, 0:256], 127.0, -128.0, ALU.min, ALU.max)
                            v.tensor_scalar(fqb[:, b:b + 256], big8f[:, 256:512], 127.0, -128.0,
                                            ALU.min, ALU.max).then_inc(sDV.h, 1)
                        return f
                    em("vector", mk_fq(q, n2, AM2BCEV_DV))
                    fq_dv.append(sDV.bump(1))
                FQ_DONE_DV = sDV.n
                q_state["prev_fq_pe"] = 0

                dn_pe = []
                for grp in range(2):
                    dtl = []
                    for kch in range(64):
                        def mk_dtl(q, grp, kch, prev_pe):
                            def f(e):
                                if kch == 0 and grp == 0 and q == 0:
                                    e.wait_ge(sKV.h, FQ_AG)
                                if prev_pe:
                                    e.wait_ge(sPE.h, prev_pe)
                                wslot = wec[kch % 2]
                                e.dma_start(out=wslot[:, 0:1024]
                                            .rearrange("p (a c) -> p a c", c=128),
                                            in_=fqd_full[kch // 8, kch % 8, grp, :, :, :]
                                            .rearrange("a p c -> p a c")).then_inc(sDT.h, 16)
                            return f
                        prev_pe = dn_pe[-2] if len(dn_pe) >= 2 else q_state["prev_dtl_pe"]
                        em("sync", mk_dtl(q, grp, kch, prev_pe))
                        DT_AT = sDT.bump(16)

                        prev_ev = x2_dv[-1] if (grp == 1 and x2_dv) else 0

                        def mk_dnmm(q, grp, kch, DT_AT, FQ_DONE_DV,
                                    fqb=fqb, AM2BCEV_DV=AM2BCEV_DV, prev_ev=prev_ev):
                            def f(t):
                                t.wait_ge(sDT.h, DT_AT)
                                if kch == 0:
                                    t.wait_ge(sDV.h, FQ_DONE_DV)
                                    if grp == 0:
                                        t.wait_ge(sDV.h, AM2BCEV_DV)
                                    else:
                                        t.wait_ge(sDV.h, prev_ev)
                                wslot = wec[kch % 2]
                                for m8 in range(8):
                                    ins = t.matmul(pbank[m8][:, 0:256],
                                                   wslot[:, m8 * 128:(m8 + 1) * 128],
                                                   fqb[:, kch * 256:(kch + 1) * 256],
                                                   start=(kch == 0), stop=(kch == 63))
                                ins.then_inc(sPE.h, 1)
                            return f
                        em("tensor", mk_dnmm(q, grp, kch, DT_AT, FQ_DONE_DV))
                        dn_pe.append(sPE.bump(1))
                    DN_PE = sPE.n

                    def mk_dnev(q, grp, DN_PE):
                        def f(v):
                            v.wait_ge(sPE.h, DN_PE)
                            for m8 in range(8):
                                v.tensor_tensor(big8f[:, m8 * 256:(m8 + 1) * 256],
                                                pbank[m8][:, 0:256], bcI2, ALU.mult)
                            for m8 in range(8):
                                mch = grp * 8 + m8
                                ins = v.scalar_tensor_tensor(
                                    X[:, mch * TL + q * 256: mch * TL + q * 256 + 256],
                                    big8f[:, m8 * 256:(m8 + 1) * 256],
                                    modv[:, 80 + mch:80 + mch + 1],
                                    X[:, mch * TL + q * 256: mch * TL + q * 256 + 256],
                                    ALU.mult, ALU.add)
                            ins.then_inc(sDV.h, 1)
                        return f
                    em("vector", mk_dnev(q, grp, DN_PE))
                    x2_dv.append(sDV.bump(1))
                q_state["prev_dtl_pe"] = dn_pe[-1]
                q_state["prev_x2"] = x2_dv[-1]
            X2_DONE = sDV.n
        else:
            X2_DONE = X1_DONE


        if KCUT >= 7:
            sSZ = sem("sSZ"); sZL = sem("sZL")
            KVH = 4
            LB = 96
            kem4 = scrb[:, 0:4096]
            vem4 = scrb[:, 4096:8192]
            kftok = scrb[:, 8192:12288]
            vftok = vem4
            costb = shared4k[:, 0:TL]
            sintb = shared4k[:, TL:2 * TL]
            denf = den[:].bitcast(F32)
            spf = [denf[:, 0:129], denf[:, 132:261], denf[:, 264:393],
                   qin[:, 400:529]]
            spfb = [qstage[:, k * 132:k * 132 + 129] for k in range(KVH)]

            ln_state["prev_stat_free"] = x2_dv[-1]
            H1A_DONE = ln_site(X2_DONE, LB + 16, LB + 0, hpan)

            def p_riota(g):
                g.wait_ge(sDV.h, X2_DONE)
                g.iota(qin[:, 0:64], pattern=[[1, 64]], base=0, channel_multiplier=0,
                       allow_small_or_imprecise_dtypes=True).then_inc(sGP.h, 1)
                g.iota(qin[:, 64:65], pattern=[[0, 1]], base=0, channel_multiplier=1,
                       allow_small_or_imprecise_dtypes=True).then_inc(sGP.h, 1)
            em("gpsimd", p_riota)
            GP_RIOTA = sGP.bump(2)

            def p_rinvf(s):
                s.wait_ge(sGP.h, GP_RIOTA)
                s.activation(qin[:, 128:192], qin[:, 0:64], AF.Exp,
                             bias=0.0, scale=-float(np.log(10000.0) / 64.0)).then_inc(sAC.h, 1)
            em("scalar", p_rinvf)
            AC_RINVF = sAC.bump(1)

            TWO_PI = float(2.0 * np.pi)
            for tb in range(8):
                def mk_rt(tb, B=sDV.n):
                    def f(v):
                        if tb == 0:
                            v.wait_ge(sAC.h, AC_RINVF)
                        v.tensor_scalar(qin[:, 65:66], qin[:, 64:65], float(128 * tb),
                                        toff[:, 0:1], ALU.add, ALU.add).then_inc(sDV.h, 1)
                        v.wait_ge(sDV.h, B + 1)
                        v.tensor_scalar(big8f[:, 0:64], qin[:, 128:192], qin[:, 65:66],
                                        1.0 / TWO_PI, ALU.mult, ALU.mult).then_inc(sDV.h, 1)
                        v.wait_ge(sDV.h, B + 2)
                        v.tensor_scalar(big8f[:, 64:128], big8f[:, 0:64], 0.25,
                                        None, ALU.add).then_inc(sDV.h, 1)
                        v.wait_ge(sDV.h, B + 3)
                        v.tensor_scalar(big8f[:, 128:256], big8f[:, 0:128],
                                        MAGIC - 0.5, None, ALU.add).then_inc(sDV.h, 1)
                        v.wait_ge(sDV.h, B + 4)
                        v.tensor_scalar(big8f[:, 128:256], big8f[:, 128:256],
                                        MAGIC, None, ALU.subtract).then_inc(sDV.h, 1)
                        v.wait_ge(sDV.h, B + 5)
                        v.scalar_tensor_tensor(big8f[:, 256:384], big8f[:, 128:256], -1.0,
                                               big8f[:, 0:128], ALU.mult, ALU.add).then_inc(sDV.h, 1)
                        v.wait_ge(sDV.h, B + 6)
                        v.tensor_scalar(big8f[:, 384:512], big8f[:, 256:384], 0.5,
                                        None, ALU.is_ge).then_inc(sDV.h, 1)
                        v.wait_ge(sDV.h, B + 7)
                        v.scalar_tensor_tensor(big8f[:, 256:384], big8f[:, 384:512], -1.0,
                                               big8f[:, 256:384], ALU.mult, ALU.add).then_inc(sDV.h, 1)
                    return f
                em("vector", mk_rt(tb))
                rt_at = sDV.bump(8)

                def mk_rt2(tb, rt_at=rt_at):
                    def f(s):
                        s.wait_ge(sDV.h, rt_at)
                        s.activation(big8f[:, 384:512], big8f[:, 256:384], AF.Sin,
                                     bias=0.0, scale=TWO_PI).then_inc(sAC.h, 1)
                    return f
                em("scalar", mk_rt2(tb))
                rt2_at = sAC.bump(1)

                def mk_rt3(tb, rt2_at=rt2_at):
                    def f(v):
                        v.wait_ge(sAC.h, rt2_at)
                        v.tensor_copy(sintb[0:128, tb * 128 + 64:tb * 128 + 128],
                                      big8f[:, 384:448])
                        v.tensor_copy(costb[0:128, tb * 128:tb * 128 + 64],
                                      big8f[:, 448:512])
                        v.tensor_copy(costb[0:128, tb * 128 + 64:tb * 128 + 128],
                                      big8f[:, 448:512])
                        v.tensor_scalar(sintb[0:128, tb * 128:tb * 128 + 64],
                                        big8f[:, 384:448], -1.0, None,
                                        ALU.mult).then_inc(sDV.h, 1)
                    return f
                em("vector", mk_rt3(tb))
                sDV.bump(1)
            ROPE_TBL = sDV.n

            def q1_out(ech, pp, pe_at):
                def f(v, ech=ech, pp=pp, pe_at=pe_at):
                    v.wait_ge(sPE.h, pe_at)
                    v.tensor_copy(qem[:, ech * TL:(ech + 1) * TL], pp[:]).then_inc(sDV.h, 1)
                em("vector", f)
                sDV.bump(1)
            Q1_PE = proj_em(hpan, l1wq, NF, q1_out, H1A_DONE,
                            psum_free=(sDV, X2_DONE))
            Q1_DONE = sDV.n

            def k1_out(ech, pp, pe_at):
                def f(v, ech=ech, pp=pp, pe_at=pe_at):
                    v.wait_ge(sPE.h, pe_at)
                    v.tensor_copy(kem4[:, ech * TL:(ech + 1) * TL], pp[:]).then_inc(sDV.h, 1)
                em("vector", f)
                sDV.bump(1)
            K1_PE = proj_em(hpan, l1wk, KVH, k1_out, H1A_DONE,
                            psum_free=(sDV, Q1_DONE), wbuf_free_pe=Q1_PE)
            K1_DONE = sDV.n

            def v1_out(ech, pp, pe_at):
                def f(v, ech=ech, pp=pp, pe_at=pe_at):
                    v.wait_ge(sPE.h, pe_at)
                    v.tensor_copy(vem4[:, ech * TL:(ech + 1) * TL], pp[:]).then_inc(sDV.h, 1)
                em("vector", f)
                sDV.bump(1)
            V1_PE = proj_em(hpan, l1wv, KVH, v1_out, H1A_DONE,
                            psum_free=(sDV, K1_DONE), wbuf_free_pe=K1_PE)
            V1_DONE = sDV.n

            rope_state = {"prev_ev": 0}

            def rope_panel(src_panel, em_dst, tok_dst, src_dv, extra_dv=0):
                EV0 = sDV.n
                prev_ev = rope_state["prev_ev"]

                def f_tr(t, EV0=EV0, prev_ev=prev_ev, src_dv=src_dv, extra_dv=extra_dv):
                    t.wait_ge(sDV.h, src_dv)
                    if extra_dv:
                        t.wait_ge(sDV.h, extra_dv)
                    if prev_ev:
                        t.wait_ge(sDV.h, prev_ev)
                    for tb in range(8):
                        if tb >= 2:
                            t.wait_ge(sDV.h, EV0 + tb - 1)
                        pp = pD if tb % 2 == 0 else pE
                        t.transpose(pp[:].bitcast(BF16)[:, 0:128],
                                    src_panel[:, tb * 128:(tb + 1) * 128],
                                    ident[:]).then_inc(sPE.h, 1)
                em("tensor", f_tr)
                tr_pe = sPE.bump(8)

                def f_ev(v, tr_pe=tr_pe):
                    for tb in range(8):
                        v.wait_ge(sPE.h, tr_pe - 7 + tb)
                        pp = pD if tb % 2 == 0 else pE
                        v.tensor_copy(big8b[:, tb * 128:(tb + 1) * 128],
                                      pp[:].bitcast(BF16)[:, 0:128]).then_inc(sDV.h, 1)
                em("vector", f_ev)
                ev_dv = sDV.bump(8)

                def f_rope(v, B=sDV.n):
                    v.wait_ge(sDV.h, ev_dv)
                    v.wait_ge(sDV.h, ROPE_TBL)
                    v.tensor_tensor(
                        big8b[:, 2048:3072].rearrange("p (b s) -> p b s", s=128)[:, :, 0:64],
                        big8b[:, 0:1024].rearrange("p (b s) -> p b s", s=128)[:, :, 64:128],
                        sintb[0:128, :].rearrange("p (b s) -> p b s", s=128)[:, :, 0:64],
                        ALU.mult)
                    v.tensor_tensor(big8b[:, 1024:2048], big8b[:, 0:1024],
                                    costb[0:128, :], ALU.mult)
                    v.tensor_tensor(
                        big8b[:, 2048:3072].rearrange("p (b s) -> p b s", s=128)[:, :, 64:128],
                        big8b[:, 0:1024].rearrange("p (b s) -> p b s", s=128)[:, :, 0:64],
                        sintb[0:128, :].rearrange("p (b s) -> p b s", s=128)[:, :, 64:128],
                        ALU.mult).then_inc(sDV.h, 1)
                    v.wait_ge(sDV.h, B + 1)
                    v.tensor_tensor(big8b[:, 3072:4096], big8b[:, 1024:2048],
                                    big8b[:, 2048:3072], ALU.add).then_inc(sDV.h, 1)
                    v.wait_ge(sDV.h, B + 2)
                    v.tensor_scalar(big8b[:, 1024:2048], big8b[:, 3072:4096], 0.0,
                                    None, ALU.min).then_inc(sDV.h, 1)
                em("vector", f_rope)
                rp_dv = sDV.bump(3)

                def f_elu(s, rp_dv=rp_dv):
                    s.wait_ge(sDV.h, rp_dv)
                    s.activation(big8b[:, 2048:3072], big8b[:, 1024:2048],
                                 AF.Exp).then_inc(sAC.h, 1)
                em("scalar", f_elu)
                elu_ac = sAC.bump(1)

                def f_elu2(v, elu_ac=elu_ac):
                    v.wait_ge(sAC.h, elu_ac)
                    v.scalar_tensor_tensor(big8b[:, 0:1024], big8b[:, 3072:4096], 0.0,
                                           big8b[:, 2048:3072], ALU.max,
                                           ALU.add).then_inc(sDV.h, 1)
                em("vector", f_elu2)
                E2 = sDV.bump(1)
                tok_dv = 0
                if tok_dst is not None:
                    def f_tok(v, E2=E2, tok_dst=tok_dst):
                        v.wait_ge(sDV.h, E2)
                        v.tensor_copy(tok_dst, big8b[:, 0:1024]).then_inc(sDV.h, 1)
                    em("vector", f_tok)
                    tok_dv = sDV.bump(1)
                SRC_DV = sDV.n
                EV2_0 = sDV.n

                def f_tr2(t, EV2_0=EV2_0, SRC_DV=SRC_DV):
                    t.wait_ge(sDV.h, SRC_DV)
                    for tb in range(8):
                        if tb >= 2:
                            t.wait_ge(sDV.h, EV2_0 + tb - 1)
                        pp = pD if tb % 2 == 0 else pE
                        t.transpose(pp[:].bitcast(BF16)[:, 0:128],
                                    big8b[:, tb * 128:(tb + 1) * 128],
                                    ident[:]).then_inc(sPE.h, 1)
                em("tensor", f_tr2)
                tr2_pe = sPE.bump(8)

                def f_ev2(v, tr2_pe=tr2_pe, em_dst=em_dst):
                    for tb in range(8):
                        v.wait_ge(sPE.h, tr2_pe - 7 + tb)
                        pp = pD if tb % 2 == 0 else pE
                        v.tensor_copy(em_dst[:, tb * 128:(tb + 1) * 128],
                                      pp[:].bitcast(BF16)[:, 0:128]).then_inc(sDV.h, 1)
                em("vector", f_ev2)
                em_dv = sDV.bump(8)
                rope_state["prev_ev"] = em_dv
                return em_dv, tok_dv

            kf_dv = []
            for kh in range(KVH):
                pan = kem4[:, kh * TL:(kh + 1) * TL]
                em_dv, tok_dv = rope_panel(pan, pan, kftok[:, kh * TL:(kh + 1) * TL],
                                           K1_DONE, extra_dv=ROPE_TBL if kh == 0 else 0)
                kf_dv.append((em_dv, tok_dv))
            KF_DONE = sDV.n

            vf_dv = []
            for kh in range(KVH):
                EV0 = sDV.n
                prev_ev = rope_state["prev_ev"]

                def f_vtr(t, kh=kh, EV0=EV0, prev_ev=prev_ev):
                    if kh == 0:
                        t.wait_ge(sDV.h, V1_DONE)
                    t.wait_ge(sDV.h, prev_ev)
                    for tb in range(8):
                        if tb >= 2:
                            t.wait_ge(sDV.h, EV0 + tb - 1)
                        pp = pD if tb % 2 == 0 else pE
                        t.transpose(pp[:].bitcast(BF16)[:, 0:128],
                                    vem4[:, kh * TL + tb * 128:kh * TL + tb * 128 + 128],
                                    ident[:]).then_inc(sPE.h, 1)
                em("tensor", f_vtr)
                vtr1_pe = sPE.bump(8)

                def f_vev(v, kh=kh, vtr1_pe=vtr1_pe):
                    for tb in range(8):
                        v.wait_ge(sPE.h, vtr1_pe - 7 + tb)
                        pp = pD if tb % 2 == 0 else pE
                        v.tensor_copy(vem4[:, kh * TL + tb * 128:kh * TL + tb * 128 + 128],
                                      pp[:].bitcast(BF16)[:, 0:128]).then_inc(sDV.h, 1)
                em("vector", f_vev)
                vf_dv.append(sDV.bump(8))
                rope_state["prev_ev"] = vf_dv[-1]
            VF_DONE = sDV.n

            for h in range(H):
                pan = qem[:, h * TL:(h + 1) * TL]
                rope_panel(pan, pan, None, Q1_DONE)
            QF_DONE = sDV.n

            szev_dv = []
            for kh in range(KVH):
                def f_sz(t, kh=kh):
                    t.wait_ge(sDV.h, kf_dv[kh][1])
                    t.wait_ge(sDV.h, vf_dv[kh])
                    if kh >= 1:
                        t.wait_ge(sDV.h, szev_dv[kh - 1])
                    for tb in range(8):
                        c0 = kh * TL + tb * 128
                        t.matmul(pC[:, 0:128], kftok[:, c0:c0 + 128],
                                 vftok[:, c0:c0 + 128], start=(tb == 0), stop=False,
                                 skip_group_check=True)
                        ins = t.matmul(pC[:, 128:129], kftok[:, c0:c0 + 128],
                                       ones_col[:], start=False, stop=(tb == 7),
                                       skip_group_check=True)
                    ins.then_inc(sPE.h, 1)
                em("tensor", f_sz)
                sz_pe = sPE.bump(1)

                def f_szev(v, kh=kh, sz_pe=sz_pe):
                    v.wait_ge(sPE.h, sz_pe)
                    v.wait_ge(sSZ.h, 16 * kh)
                    v.tensor_copy(qin[:, 0:129], pC[:, 0:129]).then_inc(sDV.h, 1)
                em("vector", f_szev)
                szev_dv.append(sDV.bump(1))

                def f_szpub(g, kh=kh, ev=szev_dv[-1]):
                    g.wait_ge(sDV.h, ev)
                    g.dma_start(out=sz_sh[kh * 128:(kh + 1) * 128, 0:129],
                                in_=qin[:, 0:129]).then_inc(sSZ.h, 16)
                em("gpsimd", f_szpub)
                sSZ.bump(16)
            SZ_PUB = sSZ.n

            def p_ag_sz(g):
                g.wait_ge(sSZ.h, SZ_PUB)
                g.collective_compute("AllGather", ALU.bypass, replica_groups=RG,
                                     ins=[sz_sh.ap()], outs=[sz_full.ap()]).then_inc(sKV.h, 1)
            em("gpsimd", p_ag_sz)
            SZ_AG = sKV.bump(1)

            zl_dv = []
            for kh in range(KVH):
                def f_szl(e, kh=kh, prev=zl_dv[-1] if zl_dv else 0):
                    e.wait_ge(sKV.h, SZ_AG)
                    if kh == 0:
                        e.wait_ge(sDV.h, QF_DONE)
                    else:
                        e.wait_ge(sDV.h, prev)
                    for r in range(NCORES):
                        e.dma_start(out=big8f[:, r * 136:r * 136 + 129],
                                    in_=sz_full[r, kh * 128:(kh + 1) * 128, 0:129]).then_inc(sZL.h, 16)
                em("sync", f_szl)
                ZL_AT = sZL.bump(16 * 8)

                def f_pref(v, kh=kh, ZL_AT=ZL_AT, B=sDV.n):
                    v.wait_ge(sZL.h, ZL_AT)
                    v.tensor_scalar(spf[kh], big8f[:, 0:129], pm[:, 0:1],
                                    None, ALU.mult)
                    v.tensor_scalar(qin[:, 0:129], big8f[:, 136:136 + 129],
                                    pm[:, 1:2], None, ALU.mult)
                    for r in range(2, NCORES, 2):
                        v.scalar_tensor_tensor(spf[kh], big8f[:, r * 136:r * 136 + 129],
                                               pm[:, r:r + 1], spf[kh],
                                               ALU.mult, ALU.add)
                        ins = v.scalar_tensor_tensor(qin[:, 0:129],
                                                     big8f[:, (r + 1) * 136:(r + 1) * 136 + 129],
                                                     pm[:, r + 1:r + 2], qin[:, 0:129],
                                                     ALU.mult, ALU.add)
                    ins.then_inc(sDV.h, 1)
                    v.wait_ge(sDV.h, B + 1)
                    v.tensor_tensor(spf[kh], spf[kh], qin[:, 0:129],
                                    ALU.add).then_inc(sDV.h, 1)
                em("vector", f_pref)
                zl_dv.append(sDV.bump(2))
            PREF_DONE = sDV.n

            abuf = [qinb[:, 0:128], qinb[:, 128:256]]
            obuf = [qinb[:, 256:384], qinb[:, 384:512]]
            dcol = qin[:, 256:264]
            banks1 = [pA[:, 0:512], pA[:, 512:1024], pB[:, 0:512], pB[:, 512:1024]]
            out1_dv = []
            am_dv = []
            it = 0
            for kh in range(KVH):
                def f_st0(v, kh=kh, B=PREF_DONE):
                    v.wait_ge(sDV.h, B)
                    v.tensor_copy(spfb[kh], spf[kh]).then_inc(sDV.h, 1)
                em("vector", f_st0)
                st_dv = sDV.bump(1)
                for j in range(8):
                    for hi in range(4):
                        h = kh * 4 + hi
                        ab = it % 2
                        ob = it % 2
                        bank = banks1[it % 4]
                        tslc = slice(kh * TL + j * 128, kh * TL + j * 128 + 128)
                        qslc = slice(h * TL + j * 128, h * TL + j * 128 + 128)

                        def f_A(t, tslc=tslc, qslc=qslc, st_dv=st_dv,
                                prev_am=am_dv[-1] if am_dv else 0,
                                bank_free=out1_dv[-4] if len(out1_dv) >= 4 else 0):
                            t.wait_ge(sDV.h, st_dv)
                            if prev_am:
                                t.wait_ge(sDV.h, prev_am)
                            if bank_free:
                                t.wait_ge(sDV.h, bank_free)
                            t.matmul(pE[:, 0:128], kem4[:, tslc], qem[:, qslc],
                                     start=True, stop=True).then_inc(sPE.h, 1)
                        em("tensor", f_A)
                        A_PE = sPE.bump(1)

                        def f_Am(v, A_PE=A_PE, ab=ab):
                            v.wait_ge(sPE.h, A_PE)
                            v.tensor_tensor(abuf[ab], pE[:, 0:128], tri[:],
                                            ALU.mult).then_inc(sDV.h, 1)
                        em("vector", f_Am)
                        am_dv.append(sDV.bump(1))

                        def f_out(t, kh=kh, tslc=tslc, qslc=qslc, AM_DV=am_dv[-1],
                                  ab=ab, bank=bank):
                            t.wait_ge(sDV.h, AM_DV)
                            t.matmul(bank[:, 0:129], qem[:, qslc], spfb[kh],
                                     start=True, stop=False, skip_group_check=True)
                            t.matmul(bank[:, 0:128], abuf[ab], vftok[:, tslc],
                                     start=False, stop=False, skip_group_check=True)
                            t.matmul(bank[:, 128:129], abuf[ab], ones_col[:],
                                     start=False, stop=True,
                                     skip_group_check=True).then_inc(sPE.h, 1)
                        em("tensor", f_out)
                        OUT_PE = sPE.bump(1)

                        def f_oev(v, OUT_PE=OUT_PE, ob=ob, bank=bank, B=sDV.n):
                            v.wait_ge(sPE.h, OUT_PE)
                            v.tensor_scalar(dcol[:, 0:1], bank[:, 128:129], 1e-6,
                                            None, ALU.add).then_inc(sDV.h, 1)
                            v.wait_ge(sDV.h, B + 1)
                            v.reciprocal(dcol[:, 1:2], dcol[:, 0:1]).then_inc(sDV.h, 1)
                            v.wait_ge(sDV.h, B + 2)
                            v.tensor_scalar(obuf[ob], bank[:, 0:128], dcol[:, 1:2],
                                            None, ALU.mult).then_inc(sDV.h, 1)
                        em("vector", f_oev)
                        OEV_DV = sDV.bump(3)

                        def f_otr(t, OEV_DV=OEV_DV, ob=ob, it=it,
                                  pd_free=out1_dv[-2] if len(out1_dv) >= 2 else 0):
                            t.wait_ge(sDV.h, OEV_DV)
                            if pd_free:
                                t.wait_ge(sDV.h, pd_free)
                            slot = 256 + (it % 2) * 128
                            t.transpose(pD[:].bitcast(BF16)[:, slot:slot + 128],
                                        obuf[ob], ident[:]).then_inc(sPE.h, 1)
                        em("tensor", f_otr)
                        OTR_PE = sPE.bump(1)

                        def f_oevt(v, OTR_PE=OTR_PE, qslc=qslc, it=it):
                            v.wait_ge(sPE.h, OTR_PE)
                            slot = 256 + (it % 2) * 128
                            v.tensor_copy(hpan[:, qslc],
                                          pD[:].bitcast(BF16)[:, slot:slot + 128]).then_inc(sDV.h, 1)
                        em("vector", f_oevt)
                        out1_dv.append(sDV.bump(1))
                        it += 1
                    if j < 7:
                        def f_sup(t, kh=kh, j=j):
                            for tb2 in (j,):
                                c0 = kh * TL + tb2 * 128
                                t.matmul(pC[:, 0:128], kftok[:, c0:c0 + 128],
                                         vftok[:, c0:c0 + 128], start=True, stop=False,
                                         skip_group_check=True)
                                t.matmul(pC[:, 128:129], kftok[:, c0:c0 + 128],
                                         ones_col[:], start=False, stop=True,
                                         skip_group_check=True).then_inc(sPE.h, 1)
                        em("tensor", f_sup)
                        SUP_PE = sPE.bump(1)

                        def f_supd(v, kh=kh, SUP_PE=SUP_PE, B=sDV.n):
                            v.wait_ge(sPE.h, SUP_PE)
                            v.tensor_tensor(spf[kh], spf[kh], pC[:, 0:129],
                                            ALU.add).then_inc(sDV.h, 1)
                            v.wait_ge(sDV.h, B + 1)
                            v.tensor_copy(spfb[kh], spf[kh]).then_inc(sDV.h, 1)
                        em("vector", f_supd)
                        st_dv = sDV.bump(2)
            OUT1_DONE = sDV.n

            def o1_out(ech, pp, pe_at):
                def f(v, ech=ech, pp=pp, pe_at=pe_at):
                    v.wait_ge(sPE.h, pe_at)
                    v.scalar_tensor_tensor(X[:, ech * TL:(ech + 1) * TL], pp[:],
                                           modv[:, 128 + ech:128 + ech + 1],
                                           X[:, ech * TL:(ech + 1) * TL],
                                           ALU.mult, ALU.add).then_inc(sDV.h, 1)
                em("vector", f)
                sDV.bump(1)
            O1P = proj_em(hpan, l1wo, NF, o1_out, OUT1_DONE,
                          psum_free=(sDV, OUT1_DONE), wbuf_free_pe=V1_PE)
            X3_DONE = sDV.n
        else:
            X3_DONE = X2_DONE

        if KCUT >= 8:
            sHB = sem("sHB"); sPK = sem("sPK"); sDQ = sem("sDQ")
            sDQr = sem("sDQr"); sDQs = sem("sDQs"); sQO = sem("sQO")
            sRH = sem("sRH"); sPL = sem("sPL"); sRS = sem("sRS")
            sDWp = sem("sDWp"); sDWb = sem("sDWb")

            ln_state["prev_stat_free"] = X3_DONE
            H1B1_DONE = ln_site(X3_DONE, LB + 64, LB + 48, hpan)

            def p_hbpub(e):
                e.wait_ge(sDV.h, H1B1_DONE)
                e.dma_start(out=hb_sh[:, :].rearrange("(c p) t -> p c t", p=128),
                            in_=hpan[:].rearrange("p (c t) -> p c t",
                                                  t=TL)).then_inc(sHB.h, 16)
            em("sync", p_hbpub)
            HB_PUB = sHB.bump(16)

            def p_hb_ag(g):
                g.wait_ge(sHB.h, HB_PUB)
                g.collective_compute("AllGather", ALU.bypass, replica_groups=RG,
                                     ins=[hb_sh.ap()], outs=[hb_full.ap()]).then_inc(sKV.h, 1)
            em("gpsimd", p_hb_ag)
            HB_AG = sKV.bump(1)

            srow0i = srow0[:].bitcast(I32)
            srow1i = srow1[:].bitcast(I32)
            stg = [srow0i, srow1i]
            big8i = big8[:].bitcast(I32)
            I0 = big8i[:, 0:1024]; I1 = big8i[:, 1024:2048]
            TA = qstage[:]; TB = xsq[:]; TC2 = rrowb[:]
            TD = den[:, 0:1024]
            TE = big8b[:, 2048:3072]; TF = big8b[:, 3072:4096]
            SCL = shared4k[:].bitcast(F32)[:, 0:1024]
            Q4 = qinb
            W_PK = [l1gp_s, l1up_s, l1dp_s]
            W_SC = [l1gs_s, l1us_s, l1ds_s]
            W_DW = [l1gw, l1uw, l1dw]

            def mk_strip(s):
                w4, r4_ = divmod(s, 16)
                h2, P = divmod(r4_, 8)
                pk = W_PK[w4]; sc = W_SC[w4]; dw = W_DW[w4]
                st = stg[s % 2]

                def f_ld(g, s=s, pk=pk, P=P, h2=h2, st=st, sc=sc):
                    if s == 0:
                        g.wait_ge(sDV.h, H1B1_DONE)
                    if s >= 2:
                        g.wait_ge(sDQr.h, s - 1)
                    g.dma_start(out=st,
                                in_=pk[128 * P:128 * P + 128,
                                       1024 * h2:1024 * h2 + 1024]).then_inc(sPK.h, 16)
                    if s >= 1:
                        g.wait_ge(sDQs.h, s)
                    g.dma_start(out=SCL, in_=sc[h2 * 8 + P, :, :]).then_inc(sPK.h, 16)

                def f_dq(v, s=s, st=st):
                    v.wait_ge(sPK.h, 32 * (s + 1))
                    v.tensor_scalar(I0, st, 15, None, ALU.bitwise_and)
                    v.tensor_scalar(I1, st, 4, None,
                                    ALU.logical_shift_right).then_inc(sDQr.h, 1)
                    v.tensor_scalar(TA, I0, 1, None, ALU.mult)
                    v.tensor_scalar(TB, I1, 1, None, ALU.mult)
                    v.tensor_scalar(TC2, TA, 7.5, None, ALU.is_ge)
                    v.tensor_scalar(TD, TB, 7.5, None, ALU.is_ge)
                    v.scalar_tensor_tensor(TA, TC2, -8.0, TA, ALU.mult, ALU.add)
                    v.scalar_tensor_tensor(TB, TD, -8.0, TB, ALU.mult, ALU.add)
                    v.tensor_scalar(TC2, TC2, -2.0, 1.0, ALU.mult, ALU.add)
                    v.tensor_scalar(TD, TD, -2.0, 1.0, ALU.mult, ALU.add)
                    v.tensor_tensor(TC2, TC2, SCL, ALU.mult)
                    v.tensor_tensor(TD, TD, SCL, ALU.mult).then_inc(sDQs.h, 1)
                    v.tensor_scalar(TE, TA, -6.0, 0.0, ALU.add, ALU.max)
                    v.tensor_scalar(TF, TB, -6.0, 0.0, ALU.add, ALU.max)
                    v.scalar_tensor_tensor(TE, TA, 0.5, TE, ALU.mult, ALU.add)
                    v.scalar_tensor_tensor(TF, TB, 0.5, TF, ALU.mult, ALU.add)
                    v.tensor_scalar(TA, TA, -4.0, 0.0, ALU.add, ALU.max)
                    v.tensor_scalar(TB, TB, -4.0, 0.0, ALU.add, ALU.max)
                    v.scalar_tensor_tensor(TE, TA, 0.5, TE, ALU.mult, ALU.add)
                    v.scalar_tensor_tensor(TF, TB, 0.5, TF, ALU.mult, ALU.add)
                    if s >= 1:
                        v.wait_ge(sQO.h, 32 * s)
                    v.tensor_tensor(Q4[:, 0:1024], TE, TC2, ALU.mult)
                    v.tensor_tensor(Q4[:, 1024:2048], TF, TD,
                                    ALU.mult).then_inc(sDQ.h, 1)

                def f_qo(e, s=s, dw=dw, P=P, h2=h2):
                    e.wait_ge(sDQ.h, s + 1)
                    e.dma_start(out=dw[P, 8 * h2:8 * h2 + 8, :, :].rearrange("n p c -> p n c"),
                                in_=Q4[:, 0:1024].rearrange("p (n c) -> p n c",
                                                            c=128)).then_inc(sQO.h, 16)
                    e.dma_start(out=dw[8 + P, 8 * h2:8 * h2 + 8, :, :].rearrange("n p c -> p n c"),
                                in_=Q4[:, 1024:2048].rearrange("p (n c) -> p n c",
                                                               c=128)).then_inc(sQO.h, 16)
                return f_ld, f_dq, f_qo

            strips4 = [mk_strip(s) for s in range(48)]
            sPK.bump(32 * 48); sDQ.bump(48); sDQr.bump(48)
            sDQs.bump(48); sQO.bump(32 * 48)
            drain4 = {"n": 0}

            def drain_strips(k):
                for _ in range(k):
                    if drain4["n"] < 48:
                        fl, fd, fq = strips4[drain4["n"]]
                        em("sync", fl); em("vector", fd); em("scalar", fq)
                        drain4["n"] += 1

            rhs_bufs = [qarena, scrb]
            wec_hist = [O1P, O1P]
            gu4_pe = []
            ib_dv_th = {}; ib_dv = {}; ge_ac = {}; dev_dv = {}; dev_last = {}
            stg_dn = [xb[:, 0:512], xb[:, 512:1024]]
            wpar4 = {"p": 0}

            def wec_load4(eng, tiles_ap, dq_need, dsem):
                par = wpar4["p"]; wpar4["p"] ^= 1
                wb = wec[par]
                ssw = (sDWp if par == 0 else sDWb) if dsem else (sW if par == 0 else sWb)
                pe_need = wec_hist[-2]

                def f(g, wb=wb, ssw=ssw, pe_need=pe_need, tiles_ap=tiles_ap,
                      dq_need=dq_need):
                    if pe_need:
                        g.wait_ge(sPE.h, pe_need)
                    g.wait_ge(sQO.h, 32 * dq_need)
                    g.dma_start(out=wb[:].rearrange("p (k c) -> p k c", c=128),
                                in_=tiles_ap).then_inc(ssw.h, 16)
                em(eng, f)
                ssw.bump(16)
                return ssw, ssw.n

            for j in range(8):
                rhs = rhs_bufs[j % 2]

                def f_rh(e, j=j, rhs=rhs, pe_need=gu4_pe[j - 2] if j >= 2 else 0):
                    if j <= 1:
                        e.wait_ge(sKV.h, HB_AG)
                        e.wait_ge(sDV.h, OUT1_DONE)
                    else:
                        e.wait_ge(sPE.h, pe_need)
                    hbj = hb_full[j, :, :].rearrange("(c p q) t -> p c q t",
                                                     p=128, q=2)
                    e.dma_start(out=rhs[:, 0:8 * TL].rearrange("p (c t) -> p c t", t=TL),
                                in_=hbj[:, :, 0, :]).then_inc(sRH.h, 16)
                    e.dma_start(out=rhs[:, 8 * TL:16 * TL].rearrange("p (c t) -> p c t", t=TL),
                                in_=hbj[:, :, 1, :]).then_inc(sRH.h, 16)
                em("sync", f_rh)
                sRH.bump(32)

                if j == 0:
                    drain_strips(16)
                for nch in range(16):
                    bP = [pbank[(2 * nch) % 8], pbank[(2 * nch + 1) % 8]]
                    ssw, w_at = wec_load4("gpsimd",
                                          l1gw[:, nch, :, :].rearrange("k p c -> p k c"),
                                          8 if nch < 8 else 16, False)

                    def f_gmm(t, ssw=ssw, w_at=w_at, j=j, nch=nch, bP=bP, rhs=rhs,
                              acw=ge_ac.get((j, nch - 4), 0),
                              dlw=dev_last.get(j - 1, 0),
                              wb=wec[1 - wpar4["p"]]):
                        t.wait_ge(ssw.h, w_at)
                        if nch == 0:
                            t.wait_ge(sRH.h, 32 * (j + 1))
                        if acw:
                            t.wait_ge(sAC.h, acw)
                        if nch < 4 and dlw:
                            t.wait_ge(sDV.h, dlw)
                        for th in range(2):
                            for kch in range(16):
                                ins = t.matmul(
                                    bP[th][:], wb[:, kch * 128:(kch + 1) * 128],
                                    rhs[:, kch * 1024 + th * 512:kch * 1024 + th * 512 + 512],
                                    start=(kch == 0), stop=(kch == 15))
                            ins.then_inc(sPE.h, 1)
                    em("tensor", f_gmm)
                    gpe = sPE.bump(2)
                    wec_hist.append(gpe)

                    def f_gel(sc_, j=j, nch=nch, bP=bP, gpe=gpe):
                        sc_.wait_ge(sPE.h, gpe - 1)
                        if j == 0 and nch == 0:
                            sc_.wait_ge(sHB.h, HB_PUB)
                        sc_.activation(hpan[:, nch * 1024:nch * 1024 + 512], bP[0][:],
                                       AF.Gelu_apprx_tanh).then_inc(sAC.h, 1)
                        sc_.wait_ge(sPE.h, gpe)
                        sc_.activation(hpan[:, nch * 1024 + 512:nch * 1024 + 1024],
                                       bP[1][:],
                                       AF.Gelu_apprx_tanh).then_inc(sAC.h, 1)
                    em("scalar", f_gel)
                    ge_ac[(j, nch)] = sAC.bump(2)

                if j == 0:
                    drain_strips(16)
                for nch in range(16):
                    bP = [pbank[(2 * nch) % 8], pbank[(2 * nch + 1) % 8]]
                    ssw, w_at = wec_load4("gpsimd",
                                          l1uw[:, nch, :, :].rearrange("k p c -> p k c"),
                                          24 if nch < 8 else 32, False)

                    def f_umm(t, ssw=ssw, w_at=w_at, j=j, nch=nch, bP=bP, rhs=rhs,
                              ibw=ib_dv.get((j, nch - 4), 0),
                              acw=ge_ac.get((j, nch + 12), 0),
                              wb=wec[1 - wpar4["p"]]):
                        t.wait_ge(ssw.h, w_at)
                        if ibw:
                            t.wait_ge(sDV.h, ibw)
                        if nch < 4:
                            t.wait_ge(sAC.h, acw)
                        for th in range(2):
                            for kch in range(16):
                                ins = t.matmul(
                                    bP[th][:], wb[:, kch * 128:(kch + 1) * 128],
                                    rhs[:, kch * 1024 + th * 512:kch * 1024 + th * 512 + 512],
                                    start=(kch == 0), stop=(kch == 15))
                            ins.then_inc(sPE.h, 1)
                    em("tensor", f_umm)
                    upe = sPE.bump(2)
                    wec_hist.append(upe)

                    def f_tt(v, j=j, nch=nch, bP=bP, upe=upe, gac=ge_ac[(j, nch)]):
                        v.wait_ge(sAC.h, gac)
                        v.wait_ge(sPE.h, upe - 1)
                        v.tensor_tensor(hpan[:, nch * 1024:nch * 1024 + 512],
                                        hpan[:, nch * 1024:nch * 1024 + 512],
                                        bP[0][:], ALU.mult).then_inc(sDV.h, 1)
                        v.wait_ge(sPE.h, upe)
                        v.tensor_tensor(hpan[:, nch * 1024 + 512:nch * 1024 + 1024],
                                        hpan[:, nch * 1024 + 512:nch * 1024 + 1024],
                                        bP[1][:], ALU.mult).then_inc(sDV.h, 1)
                    em("vector", f_tt)
                    ib_dv_th[(j, nch, 0)] = sDV.n + 1
                    ib_dv_th[(j, nch, 1)] = sDV.n + 2
                    ib_dv[(j, nch)] = sDV.bump(2)
                gu4_pe.append(wec_hist[-1])

                if j == 0:
                    drain_strips(16)
                for hch in range(16):
                    bP = [pbank[(2 * hch) % 8], pbank[(2 * hch + 1) % 8]]
                    ssw, w_at = wec_load4("sync",
                                          l1dw[:, hch, :, :].rearrange("k p c -> p k c"),
                                          40 if hch < 8 else 48, True)

                    def f_dmm(t, ssw=ssw, w_at=w_at, j=j, hch=hch, bP=bP,
                              ibw=ib_dv.get((j, 15), 0),
                              d0=dev_dv.get((j, hch - 4, 0), 0),
                              d1=dev_dv.get((j, hch - 4, 1), 0),
                              wb=wec[1 - wpar4["p"]]):
                        t.wait_ge(ssw.h, w_at)
                        if hch == 0:
                            t.wait_ge(sDV.h, ibw)
                        if d0:
                            t.wait_ge(sDV.h, d0)
                        if d1:
                            t.wait_ge(sDV.h, d1)
                        for th in range(2):
                            for kch in range(16):
                                ins = t.matmul(
                                    bP[th][:], wb[:, kch * 128:(kch + 1) * 128],
                                    hpan[:, kch * 1024 + th * 512:kch * 1024 + th * 512 + 512],
                                    start=(kch == 0), stop=(kch == 15))
                            ins.then_inc(sPE.h, 1)
                    em("tensor", f_dmm)
                    dpe = sPE.bump(2)
                    wec_hist.append(dpe)

                    for th in range(2):
                        g4 = j * 32 + hch * 2 + th

                        def f_dev(v, dpe=dpe, th=th, bP=bP, g4=g4):
                            v.wait_ge(sPE.h, dpe - 1 + th)
                            if g4 >= 2:
                                v.wait_ge(sPL.h, 16 * (g4 - 1))
                            v.tensor_copy(stg_dn[th], bP[th][:]).then_inc(sDV.h, 1)
                        em("vector", f_dev)
                        dev_dv[(j, hch, th)] = sDV.bump(1)

                        def f_pdma(e, j=j, hch=hch, th=th, dva=dev_dv[(j, hch, th)]):
                            e.wait_ge(sDV.h, dva)
                            e.dma_start(
                                out=l1p_in[hch // 4][j, (hch % 4) * 128:(hch % 4) * 128 + 128,
                                                     th * 512:(th + 1) * 512],
                                in_=stg_dn[th]).then_inc(sPL.h, 16)
                        em("sync", f_pdma)
                        sPL.bump(16)
                dev_last[j] = dev_dv[(j, 15, 1)]

            for s4 in range(4):
                def f_rs(g, s4=s4, need=16 * (224 + 8 * (s4 + 1))):
                    g.wait_ge(sPL.h, need)
                    g.collective_compute("ReduceScatter", ALU.add, replica_groups=RG,
                                         ins=[l1p_in[s4].ap()],
                                         outs=[l1p_out[s4].ap()]).then_inc(sRS.h, 1)
                em("gpsimd", f_rs)
                sRS.bump(1)

            xadd_dv = []
            for s4 in range(4):
                def f_ol(e, s4=s4, prev=xadd_dv[-1] if xadd_dv else 0):
                    e.wait_ge(sRS.h, s4 + 1)
                    if s4 == 0:
                        e.wait_ge(sDQ.h, 48)
                    else:
                        e.wait_ge(sDV.h, prev)
                    e.dma_start(out=big8b[:, 0:4096].rearrange("p (c t) -> p c t", t=TL),
                                in_=l1p_out[s4][:, :].rearrange("(c p) t -> p c t",
                                                                p=128)).then_inc(sRH.h, 16)
                em("sync", f_ol)
                OL4 = sRH.bump(16)

                def f_xa(v, s4=s4, OL4=OL4):
                    v.wait_ge(sRH.h, OL4)
                    for i4 in range(4):
                        fc = 4 * s4 + i4
                        ins = v.scalar_tensor_tensor(
                            X[:, fc * TL:(fc + 1) * TL], big8b[:, i4 * 1024:(i4 + 1) * 1024],
                            modv[:, 176 + fc:177 + fc], X[:, fc * TL:(fc + 1) * TL],
                            ALU.mult, ALU.add)
                    ins.then_inc(sDV.h, 1)
                em("vector", f_xa)
                xadd_dv.append(sDV.bump(1))
            XF_DONE = sDV.n
        else:
            XF_DONE = X3_DONE


        if DBG8 and KCUT >= 8:
            def p_t4(e):
                e.wait_ge(sHB.h, HB_PUB)
                e.dma_start(out=t4["hb"], in_=hb_sh[:, :]).then_inc(sST.h, 16)
                e.wait_ge(sQO.h, 32 * 48)
                e.dma_start(out=t4["gw"], in_=l1gw[:, :, :, :]).then_inc(sST.h, 16)
                e.dma_start(out=t4["uw"], in_=l1uw[:, :, :, :]).then_inc(sST.h, 16)
                e.dma_start(out=t4["dw"], in_=l1dw[:, :, :, :]).then_inc(sST.h, 16)
                e.wait_ge(sRS.h, 4)
                for s5 in range(4):
                    e.dma_start(out=t4["p%d" % s5],
                                in_=l1p_out[s5][:, :]).then_inc(sST.h, 16)
                e.dma_start(out=t4["pi"].rearrange("(j r) t -> j r t", r=512),
                            in_=l1p_in[0][:, :, :]).then_inc(sST.h, 16)
            em("sync", p_t4)
            sST.bump(16 * 9)

        ep_dv = []
        for tb in range(8):
            def mk_ytr(tb, dv_need):
                def f(t):
                    if tb == 0:
                        t.wait_ge(sDV.h, XF_DONE)
                    else:
                        t.wait_ge(sDV.h, dv_need)
                    for g8 in range(2):
                        pp = pA if g8 == 0 else pB
                        for fc in range(8):
                            fcg = g8 * 8 + fc
                            ins = t.transpose(pp[:, fc * 128:(fc + 1) * 128],
                                              X[:, fcg * TL + tb * 128: fcg * TL + tb * 128 + 128],
                                              identf[:])
                        ins.then_inc(sPE.h, 1)
                return f
            em("tensor", mk_ytr(tb, sDV.n))
            pe_at = sPE.bump(2)

            def mk_yev(tb, pe_at):
                def f(v):
                    v.wait_ge(sPE.h, pe_at - 1)
                    if tb >= 1:
                        v.wait_ge(sST.h, 16 * tb)
                    v.tensor_copy(xout[:, 0:1024], pA[:])
                    v.wait_ge(sPE.h, pe_at)
                    v.tensor_copy(xout[:, 1024:2048], pB[:]).then_inc(sDV.h, 1)
                return f
            em("vector", mk_yev(tb, pe_at))
            ep_dv.append(sDV.bump(1))

            def mk_yst(tb, dv_at):
                def f(e):
                    e.wait_ge(sDV.h, dv_at)
                    e.dma_start(out=y[tb * 128:(tb + 1) * 128, :],
                                in_=xout[:]).then_inc(sST.h, 16)
                return f
            em("sync", mk_yst(tb, ep_dv[-1]))
            sST.bump(16)

        if "x1" in taps:
            def p_tap(e):
                e.wait_ge(sDV.h, X1_DONE)
                e.dma_start(out=tap_out["x1"], in_=X[:]).then_inc(sST.h, 16)
            em("sync", p_tap)
            sST.bump(16)
        if "ha" in taps:
            def p_tap2(g):
                g.wait_ge(sDV.h, H_A_DONE)
                g.dma_start(out=tap_out["ha"], in_=hpan[:]).then_inc(sST.h, 16)
            em("gpsimd", p_tap2)
            sST.bump(16)

        def p_fin(e):
            e.wait_ge(sST.h, sST.n)
        em("sync", p_fin)

        with nc.Block() as block:
            block.tensor(lambda e: [f(e) for f in E["tensor"]])
            block.vector(lambda e: [f(e) for f in E["vector"]])
            block.scalar(lambda e: [f(e) for f in E["scalar"]])
            block.gpsimd(lambda e: [f(e) for f in E["gpsimd"]])
            block.sync(lambda e: [f(e) for f in E["sync"]])
    return nc


def col128(v):
    v = np.asarray(v, np.float32).reshape(-1)
    return np.ascontiguousarray(v.reshape(-1, 128).T)


def colpan(w_t):
    i, o = w_t.shape
    kk = i // 128
    cp = w_t.reshape(kk, 128, o // 128, 128).transpose(2, 1, 0, 3)
    return np.ascontiguousarray(cp.reshape(o // 128, 128, kk * 128))


def make_in_maps(inputs):
    ct = np.ascontiguousarray
    f = np.float32
    x = np.asarray(inputs["L0_attn_x"], f).reshape(T, D)
    wq = colpan(np.asarray(inputs["L0_attn_Wq"], f).T)
    wk = colpan(np.asarray(inputs["L0_attn_Wk"], f).T)
    wv = colpan(np.asarray(inputs["L0_attn_Wv"], f).T)
    wo = colpan(np.asarray(inputs["L0_attn_Wo"], f).T)
    bq = col128(inputs["L0_attn_bq"]); bk = col128(inputs["L0_attn_bk"])
    bv = col128(inputs["L0_attn_bv"]); bo = col128(inputs["L0_attn_bo"])
    wg_t = np.asarray(inputs["L0_ffn_w_gate"], f).T
    wu_t = np.asarray(inputs["L0_ffn_w_up"], f).T
    wd_t = np.asarray(inputs["L0_ffn_w_down"], f).T
    c0 = col128(inputs["L0_c"]); c1 = col128(inputs["L1_c"])
    wm0 = np.asarray(inputs["L0_wmod"], f)
    wm1 = np.asarray(inputs["L1_wmod"], f)
    l1wq = colpan(np.asarray(inputs["L1_attn_Wq"], f).T)
    l1wk = colpan(np.asarray(inputs["L1_attn_Wk"], f).T)
    l1wv = colpan(np.asarray(inputs["L1_attn_Wv"], f).T)
    l1wo = colpan(np.asarray(inputs["L1_attn_Wo"], f).T)
    gp_t = np.asarray(inputs["L1_ffn_gate_W_packed"]).T.astype(np.int32)
    up_t = np.asarray(inputs["L1_ffn_up_W_packed"]).T.astype(np.int32)
    dp_t = np.asarray(inputs["L1_ffn_down_W_packed"]).T.astype(np.int32)
    gs_t = np.asarray(inputs["L1_ffn_gate_scales"], f).T
    us_t = np.asarray(inputs["L1_ffn_up_scales"], f).T
    ds_t = np.asarray(inputs["L1_ffn_down_scales"], f).T
    perm = np.concatenate([np.arange(0, 2048, 2), np.arange(1, 2048, 2)])

    def srep(sc):
        out = np.empty((16, 128, 1024), np.float32)
        for h2 in range(2):
            for P in range(8):
                blk = sc[8 * P:8 * P + 8, 1024 * h2:1024 * h2 + 1024]
                out[h2 * 8 + P] = np.repeat(blk, 16, axis=0)
        return ct(out)
    maps = []
    for c in range(NCORES):
        nsl = slice(2048 * c, 2048 * (c + 1))
        m = {
            "x_tok": ct(x[TL * c:TL * (c + 1)]),
            "l0wq": wq, "l0wk": wk, "l0wv": wv, "l0wo": wo,
            "l0bq": bq, "l0bk": bk, "l0bv": bv, "l0bo": bo,
            "l0c_fm": c0, "l0c_s": ct(c0[:, 2 * c:2 * c + 2]),
            "l1c_fm": c1, "l1c_s": ct(c1[:, 2 * c:2 * c + 2]),
            "wmod0_s": ct(wm0[256 * c:256 * (c + 1)]),
            "wmod1_s": ct(wm1[256 * c:256 * (c + 1)]),
            "l0wg_s": ct(wg_t[256 * c:256 * (c + 1)]),
            "l0wu_s": ct(wu_t[256 * c:256 * (c + 1)]),
            "l0wd_s": ct(wd_t[1024 * c:1024 * (c + 1)]),
            "l1wq": l1wq, "l1wk": l1wk, "l1wv": l1wv, "l1wo": l1wo,
            "l1gp_s": ct(gp_t[:, nsl][:, perm]),
            "l1up_s": ct(up_t[:, nsl][:, perm]),
            "l1dp_s": ct(dp_t[1024 * c:1024 * (c + 1)]),
            "l1gs_s": srep(gs_t[:, nsl][:, perm]),
            "l1us_s": srep(us_t[:, nsl][:, perm]),
            "l1ds_s": srep(ds_t[64 * c:64 * (c + 1)]),
            "pmask": np.broadcast_to((np.arange(NCORES) < c).astype(f),
                                     (128, NCORES)).copy(),
            "t_off": np.full((128, 1), TL * c, f),
        }
        maps.append(m)
    return maps


_NC_CACHE = {}


def kernel(**inputs):
    in_maps = make_in_maps(inputs)
    if "main" not in _NC_CACHE:
        _NC_CACHE["main"] = build()
    nc = _NC_CACHE["main"]
    res = run_bass_kernel_spmd(nc, in_maps, core_ids=list(range(NCORES)))
    out = np.concatenate([np.asarray(res.results[c]["y"], np.float32)
                          for c in range(NCORES)], axis=0)
    return out.reshape(1, T, D)
